# version 02
# baseline (speedup 1.0000x reference)
import contextlib
import numpy as np
import concourse.bass as bass
import concourse.mybir as mybir
from concourse.bass_utils import run_bass_kernel_spmd

F32 = mybir.dt.float32
BF16 = mybir.dt.bfloat16
I32 = mybir.dt.int32
AF = mybir.ActivationFunctionType
ALU = mybir.AluOpType
AX = mybir.AxisListType

B, L, DM = 2, 4096, 2048
DI, DS, DC, RK = 4096, 16, 4, 128
II = 8192
DSL = DI // 4
ISL = II // 4
NCH = 16
LC = L // NCH
TSL = LC // 4
NKB = DM // 128
NDB = DSL // 128
NIB = ISL // 128
NHB = DM // 128
RG = [[0, 1, 2, 3], [4, 5, 6, 7]]
EPS = 1e-5
DEBUG = False
_last_results = None


class Sem:
    def __init__(self, h):
        self.h = h
        self.n = 0

    def inc(self, instr, k=16):
        instr.then_inc(self.h, k)
        self.n += k
        return self.n


class Rot:

    def __init__(self, n):
        self.free_at = [0] * n
        self.ready_at = [0] * n


def build_nc():
    nc = bass.Bass("TRN2", target_bir_lowering=False, debug=False)

    d_x = nc.dram_tensor("x_b", [L, DM], F32, kind="ExternalInput").ap()
    d_winT = nc.dram_tensor("w_inT", [DM, 2048], F32, kind="ExternalInput").ap()
    d_convdiag = nc.dram_tensor("convdiag", [128, NDB, DC, 128], F32, kind="ExternalInput").ap()
    d_convb = nc.dram_tensor("convb", [128, NDB], F32, kind="ExternalInput").ap()
    d_xprojT = nc.dram_tensor("xprojT", [DSL, 160], F32, kind="ExternalInput").ap()
    d_dtprojT = nc.dram_tensor("dtprojT", [RK, DSL], F32, kind="ExternalInput").ap()
    d_dtb = nc.dram_tensor("dtb", [128, NDB], F32, kind="ExternalInput").ap()
    d_alog = nc.dram_tensor("alog", [128, NDB, DS], F32, kind="ExternalInput").ap()
    d_dvec = nc.dram_tensor("dvec", [128, NDB], F32, kind="ExternalInput").ap()
    d_opT = nc.dram_tensor("opT", [DSL, DM], F32, kind="ExternalInput").ap()
    d_gateT = nc.dram_tensor("gateT", [DM, ISL], I32, kind="ExternalInput").ap()
    d_gsc = nc.dram_tensor("gsc", [128, NIB], F32, kind="ExternalInput").ap()
    d_upT = nc.dram_tensor("upT", [DM, ISL], I32, kind="ExternalInput").ap()
    d_usc = nc.dram_tensor("usc", [128, NIB], F32, kind="ExternalInput").ap()
    d_dnT = nc.dram_tensor("dnT", [ISL, DM], I32, kind="ExternalInput").ap()
    d_dsc = nc.dram_tensor("dsc", [128, NHB], F32, kind="ExternalInput").ap()
    d_xfinT = nc.dram_tensor("xfinT", [NCH, DM, TSL], F32, kind="ExternalInput").ap()

    d_out = nc.dram_tensor("out_q", [NCH, DM, TSL], F32, kind="ExternalOutput").ap()
    dbg = {}
    if DEBUG:
        for name, shape, dt_ in [
            ("dbg_xn", [128, NKB, LC], BF16), ("dbg_xpre", [128, NDB, LC + 3], BF16),
            ("dbg_z", [128, NDB, LC], F32), ("dbg_xcf", [128, NDB, LC], F32),
            ("dbg_dt", [128, NDB, LC], F32), ("dbg_yz", [128, NDB, LC], BF16),
            ("dbg_hf", [128, NIB, LC], BF16), ("dbg_ar", [160, LC], F32),
            ("dbg_rs", [2 * DM, TSL], F32), ("dbg_fo", [128, NHB, TSL], F32),
            ("dbg_carry", [128, NDB, DS], F32), ("dbg_aneg", [128, NDB, DS], F32),
            ("dbg_xld", [128, 2, DM], F32), ("dbg_xnt", [128, 2, DM], BF16),
            ("dbg_st", [128, 4, 6], F32), ("dbg_ag4", [128, 4, 2], F32),
            ("dbg_sm", [128, 2], F32), ("dbg_var", [128, 2], F32),
            ("dbg_h", [128, 4, LC], F32),
            ("dbg_rstd", [128, 2], F32), ("dbg_nmr", [128, 2], F32),
            ("dbg_aggr", [128, 2], F32),
        ]:
            dbg[name] = nc.dram_tensor(name, shape, dt_, kind="ExternalOutput").ap()

    d_winbf = nc.dram_tensor("i_winbf", [DM, 2048], BF16, kind="Internal").ap()
    d_opbf = nc.dram_tensor("i_opbf", [DSL, DM], BF16, kind="Internal").ap()
    d_gbf = nc.dram_tensor("i_gbf", [DM, ISL], BF16, kind="Internal").ap()
    d_ubf = nc.dram_tensor("i_ubf", [DM, ISL], BF16, kind="Internal").ap()
    d_dnbf = nc.dram_tensor("i_dnbf", [ISL, DM], BF16, kind="Internal").ap()
    ar_in = [nc.dram_tensor(f"cc_ar_i{c}", [160, LC], F32, kind="Internal") for c in range(NCH)]
    ar_out = [nc.dram_tensor(f"cc_ar_o{c}", [160, LC], F32, kind="Internal") for c in range(NCH)]
    rs_in = [nc.dram_tensor(f"cc_rs_i{c}", [4 * 2 * DM, TSL], F32, kind="Internal") for c in range(NCH)]
    rs_out = [nc.dram_tensor(f"cc_rs_o{c}", [2 * DM, TSL], F32, kind="Internal") for c in range(NCH)]

    with contextlib.ExitStack() as ctx:
        E = ctx.enter_context

        s_convdiag = E(nc.sbuf_tensor("s_convdiag", [128, NDB, DC, 128], BF16))
        s_convb = E(nc.sbuf_tensor("s_convb", [128, NDB], F32))
        s_dtb2 = E(nc.sbuf_tensor("s_dtb2", [128, NDB], F32))
        s_dtbr = E(nc.sbuf_tensor("s_dtbr", [128, NDB], F32))
        s_aneg = E(nc.sbuf_tensor("s_aneg", [128, NDB, DS], F32))
        s_dvec = E(nc.sbuf_tensor("s_dvec", [128, NDB], F32))
        s_gsc = E(nc.sbuf_tensor("s_gsc", [128, NIB], F32))
        s_usc = E(nc.sbuf_tensor("s_usc", [128, NIB], F32))
        s_dsc = E(nc.sbuf_tensor("s_dsc", [128, NHB], F32))
        s_one = E(nc.sbuf_tensor("s_one", [128, 1], F32))
        s_eps = E(nc.sbuf_tensor("s_eps", [128, 1], F32))
        s_c128 = E(nc.sbuf_tensor("s_c128", [128, 1], F32))
        s_xprojT = E(nc.sbuf_tensor("s_xprojT", [128, NDB, 160], BF16))
        s_dtprojT = E(nc.sbuf_tensor("s_dtprojT", [128, DSL], BF16))
        s_xld = E(nc.sbuf_tensor("s_xld", [128, 2, DM], F32))
        s_xnt = E(nc.sbuf_tensor("s_xnt", [128, 2, DM], BF16))
        s_stats = E(nc.sbuf_tensor("s_stats", [128, 4, 6], F32))
        s_aggr = E(nc.sbuf_tensor("s_aggr", [128, 2], F32))
        s_ag4 = E(nc.sbuf_tensor("s_ag4", [128, 4, 2], F32))
        s_msq = E(nc.sbuf_tensor("s_msq", [128, 4], F32))
        s_sm = E(nc.sbuf_tensor("s_sm", [128, 2], F32))
        s_var = E(nc.sbuf_tensor("s_var", [128, 2], F32))
        s_w1 = E(nc.sbuf_tensor("s_w1", [128, 1], F32))
        s_dum = E(nc.sbuf_tensor("s_dum", [128, 2, 8], F32))
        s_w2 = E(nc.sbuf_tensor("s_w2", [128, 1], F32))
        s_rstd = E(nc.sbuf_tensor("s_rstd", [128, 2], F32))
        s_mean = E(nc.sbuf_tensor("s_mean", [128, 2], F32))
        s_nmr = E(nc.sbuf_tensor("s_nmr", [128, 2], F32))
        s_r1 = E(nc.sbuf_tensor("s_r1", [128, 1], F32))
        s_xn = E(nc.sbuf_tensor("s_xn", [128, NKB, LC], BF16))
        s_xpre = E(nc.sbuf_tensor("s_xpre", [128, NDB, LC + 3], BF16))
        s_z = E(nc.sbuf_tensor("s_z", [128, NDB, LC], F32))
        s_xcf = E(nc.sbuf_tensor("s_xcf", [128, NDB, LC], F32))
        s_xcb = E(nc.sbuf_tensor("s_xcb", [128, NDB, LC], BF16))
        s_dt = E(nc.sbuf_tensor("s_dt", [128, NDB, LC], F32))
        s_dtlow = E(nc.sbuf_tensor("s_dtlow", [128, LC], BF16))
        s_xa = E(nc.sbuf_tensor("s_xa", [128, LC], F32))
        s_xb32 = E(nc.sbuf_tensor("s_xb32", [32, LC], F32))
        s_t1 = E(nc.sbuf_tensor("s_t1", [128, LC], F32))
        s_t2 = E(nc.sbuf_tensor("s_t2", [128, LC], F32))
        s_t3 = E(nc.sbuf_tensor("s_t3", [128, LC], F32))
        s_B = E(nc.sbuf_tensor("s_B", [128, DS, LC], F32))
        s_C = E(nc.sbuf_tensor("s_C", [128, DS, LC], F32))
        s_dA = E(nc.sbuf_tensor("s_dA", [128, 2, 4, LC], F32))
        s_dBu = E(nc.sbuf_tensor("s_dBu", [128, 4, LC], F32))
        s_h = E(nc.sbuf_tensor("s_h", [128, 4, LC + 16], F32))
        s_p = E(nc.sbuf_tensor("s_p", [128, 4, LC], F32))
        s_carry = E(nc.sbuf_tensor("s_carry", [128, NDB, DS], F32))
        s_dtx = E(nc.sbuf_tensor("s_dtx", [128, LC], F32))
        s_yq = E(nc.sbuf_tensor("s_yq", [128, LC], F32))
        s_yt = E(nc.sbuf_tensor("s_yt", [128, LC], F32))
        s_yz = E(nc.sbuf_tensor("s_yz", [128, NDB, LC], BF16))
        s_hf = E(nc.sbuf_tensor("s_hf", [128, NIB, LC], BF16))
        s_sg = E(nc.sbuf_tensor("s_sg", [128, LC], F32))
        s_wslab = E(nc.sbuf_tensor("s_wslab", [128, 2, NKB, 128], BF16))
        s_wslab2 = E(nc.sbuf_tensor("s_wslab2", [128, 2, NKB, 128], BF16))
        s_dnslab = E(nc.sbuf_tensor("s_dnslab", [128, 2, 1024], BF16))
        s_opslab = E(nc.sbuf_tensor("s_opslab", [128, 2, NDB, 128], BF16))
        s_ev = E(nc.sbuf_tensor("s_ev", [128, 2, LC], F32))
        s_fs = E(nc.sbuf_tensor("s_fs", [128, NHB, TSL], F32))
        s_ff = E(nc.sbuf_tensor("s_ff", [128, NHB, TSL], F32))
        s_fg = E(nc.sbuf_tensor("s_fg", [128, NHB, TSL], F32))
        s_fx = E(nc.sbuf_tensor("s_fx", [128, NHB, TSL], F32))
        s_fo = E(nc.sbuf_tensor("s_fo", [128, NHB, TSL], F32))

        P = E(nc.psum_tensor("p_big", [128, 8, 512], F32))

        def pa(i):
            return P[:, i % 2, 0:LC]

        def pgk(i):
            return P[:, 2 + i % 2, 0:LC]

        def puk(i):
            return P[:, 4 + i % 2, 0:LC]

        def pdn(pp):
            return P[:, pp, 0:LC]

        sX = Sem(E(nc.semaphore("semX")))
        sW = Sem(E(nc.semaphore("semW")))
        sCC = Sem(E(nc.semaphore("semCC")))
        sPE = Sem(E(nc.semaphore("semPE")))
        sACT = Sem(E(nc.semaphore("semACT")))
        sDVE = Sem(E(nc.semaphore("semDVE")))
        ALLS = (sX, sW, sCC, sPE, sACT, sDVE)

        block = E(nc.Block())

        def wait_all(e):
            for s in ALLS:
                if s.n:
                    e.wait_ge(s.h, s.n)

        rot_pa = Rot(2)
        rot_gu = Rot(2)
        rot_slab = Rot(2)
        rot_dn = Rot(2)
        rot_op = Rot(2)
        rot_ev = Rot(2)
        rot_dA = Rot(2)

        def p0_consts(g):
            for dst, src in ((s_convb, d_convb), (s_dtbr, d_dtb), (s_dvec, d_dvec),
                             (s_gsc, d_gsc), (s_usc, d_usc), (s_dsc, d_dsc)):
                i = g.dma_start(out=dst[:], in_=src)
                sW.inc(i)
            i = g.dma_start(out=s_xld[:, 0, 0:NDB * DS], in_=d_alog.rearrange("p a b -> p (a b)"))
            sW.inc(i)

        block.gpsimd(p0_consts)

        def p0_dve(v):
            v.memset(s_one[:], 1.0)
            v.memset(s_eps[:], EPS)
            v.memset(s_c128[:], -128.0)
            v.memset(s_carry[:], 0.0)
            v.memset(s_xpre[:, :, 0:3], 0.0)
            i = v.memset(s_dum[:], 0.0)
            sDVE.inc(i, 1)

        block.vector(p0_dve)

        def p0_act(a):
            wait_all(a)
            a.activation(out=s_dtb2[:], in_=s_dtbr[:], func=AF.Identity, scale=2.0)
            i = a.activation(out=s_xld[:, 1, 0:NDB * DS], in_=s_xld[:, 0, 0:NDB * DS], func=AF.Exp)
            sACT.inc(i, 1)

        block.scalar(p0_act)

        def p0_dve2(v):
            wait_all(v)
            i = v.tensor_scalar_mul(s_aneg[:].rearrange("p a b -> p (a b)"),
                                    s_xld[:, 1, 0:NDB * DS], -1.0)
            sDVE.inc(i, 1)

        block.vector(p0_dve2)

        cast_jobs = [
            (d_winT, d_winbf, NKB, 2048, False),
            (d_gateT, d_gbf, NKB, ISL, True),
            (d_upT, d_ubf, NKB, ISL, True),
            (d_dnT, d_dnbf, NIB, DM, True),
            (d_opT, d_opbf, NDB, DM, False),
        ]
        cast_free = Rot(2)
        cast_nst = Rot(2)
        tcount = [0]
        for (src, dst, ntile, width, is_int) in cast_jobs:
            for t in range(ntile):
                sl = tcount[0] % 2
                tcount[0] += 1
                need_a = cast_free.free_at[sl]

                def g_load(g, src=src, t=t, sl=sl, width=width, is_int=is_int, need_a=need_a,
                           first=(tcount[0] == 1)):
                    if first:
                        wait_all(g)
                    if need_a:
                        g.wait_ge(sACT.h, need_a)
                    dstt = s_xld[:, sl, 0:width].bitcast(I32) if is_int else s_xld[:, sl, 0:width]
                    i = g.dma_start(out=dstt, in_=src[128 * t:128 * (t + 1), :])
                    sW.inc(i)

                block.gpsimd(g_load)
                need_w = sW.n
                need_st = cast_nst.free_at[sl]

                def a_cast(a, sl=sl, width=width, is_int=is_int, need_w=need_w, need_st=need_st):
                    a.wait_ge(sW.h, need_w)
                    if need_st:
                        a.wait_ge(sW.h, need_st)
                    if is_int:
                        i = a.activation(out=s_xnt[:, sl, 0:width], in_=s_xld[:, sl, 0:width].bitcast(I32),
                                         func=AF.Identity, bias=s_c128[:])
                    else:
                        i = a.activation(out=s_xnt[:, sl, 0:width], in_=s_xld[:, sl, 0:width],
                                         func=AF.Copy)
                    sACT.inc(i, 1)

                block.scalar(a_cast)
                cast_free.free_at[sl] = sACT.n
                need_a2 = sACT.n

                def g_store(g, dst=dst, t=t, sl=sl, width=width, need_a2=need_a2):
                    g.wait_ge(sACT.h, need_a2)
                    i = g.dma_start(out=dst[128 * t:128 * (t + 1), :], in_=s_xnt[:, sl, 0:width])
                    sW.inc(i)

                block.gpsimd(g_store)
                cast_nst.free_at[sl] = sW.n

        def p0_res_g(g):
            g.wait_ge(sACT.h, sACT.n)
            i = g.dma_start(out=s_xld[:, 0, 0:NDB * 160].rearrange("p (t m) -> p t m", t=NDB),
                            in_=d_xprojT.rearrange("(t p) m -> p t m", p=128))
            sW.inc(i)
            i = g.dma_start(out=s_xld[:, 1, 0:1024], in_=d_dtprojT)
            sW.inc(i)

        block.gpsimd(p0_res_g)

        def p0_res_a(a):
            a.wait_ge(sW.h, sW.n)
            a.activation(out=s_xprojT[:].rearrange("p t m -> p (t m)"),
                         in_=s_xld[:, 0, 0:NDB * 160], func=AF.Copy)
            i = a.activation(out=s_dtprojT[:], in_=s_xld[:, 1, 0:1024], func=AF.Copy)
            sACT.inc(i, 1)

        block.scalar(p0_res_a)

        def p0_res2_g(g):
            g.wait_ge(sACT.h, sACT.n)
            i = g.dma_start(out=s_xld[:].rearrange("p s m -> p (s m)"),
                            in_=d_convdiag.rearrange("p a b m -> p (a b m)"))
            sW.inc(i)

        block.gpsimd(p0_res2_g)

        def p0_res2_a(a):
            a.wait_ge(sW.h, sW.n)
            i = a.activation(out=s_convdiag[:].rearrange("p a b m -> p (a b m)"),
                             in_=s_xld[:].rearrange("p s m -> p (s m)"), func=AF.Copy)
            sACT.inc(i, 1)

        block.scalar(p0_res2_a)

        for c in range(NCH):
            tok0 = c * LC

            def ph_ln_load(g, tok0=tok0):
                wait_all(g)
                for t in range(2):
                    i = g.dma_start(out=s_xld[:, t, :],
                                    in_=d_x[tok0 + 128 * t: tok0 + 128 * (t + 1), :])
                    sW.inc(i)

            block.gpsimd(ph_ln_load)

            def ph_ln_stats(v):
                wait_all(v)
                for t in range(2):
                    for q in range(4):
                        v.bn_stats(s_stats[:, q, :], s_xld[:, t, 512 * q: 512 * (q + 1)])
                    for q in range(4):
                        i = v.bn_aggr(s_ag4[:, q, :], s_stats[:, q, :])
                    sDVE.inc(i, 1)
                    v.wait_ge(sDVE.h, sDVE.n)
                    mview = s_ag4[:, :, 0:1].rearrange("p a b -> p (a b)")
                    vview = s_ag4[:, :, 1:2].rearrange("p a b -> p (a b)")
                    v.tensor_reduce(out=s_sm[:, t: t + 1], in_=mview, axis=AX.X, op=ALU.add)
                    v.tensor_reduce(out=s_w1[:], in_=vview, axis=AX.X, op=ALU.add)
                    i = v.tensor_tensor(out=s_msq[:], in0=mview, in1=mview, op=ALU.mult)
                    sDVE.inc(i, 1)
                    v.wait_ge(sDVE.h, sDVE.n)
                    v.tensor_reduce(out=s_w2[:], in_=s_msq[:], axis=AX.X, op=ALU.add)
                    i = v.tensor_tensor(out=s_msq[:, 0:1], in0=s_sm[:, t: t + 1],
                                        in1=s_sm[:, t: t + 1], op=ALU.mult)
                    sDVE.inc(i, 1)
                    v.wait_ge(sDVE.h, sDVE.n)
                    i = v.scalar_tensor_tensor(out=s_w1[:], in0=s_msq[:, 0:1], scalar=-0.25,
                                               in1=s_w1[:], op0=ALU.mult, op1=ALU.add)
                    sDVE.inc(i, 1)
                    v.wait_ge(sDVE.h, sDVE.n)
                    i = v.scalar_tensor_tensor(out=s_var[:, t: t + 1], in0=s_w2[:], scalar=1.0,
                                               in1=s_w1[:], op0=ALU.mult, op1=ALU.add)
                    sDVE.inc(i, 1)
                    v.wait_ge(sDVE.h, sDVE.n)
                    i = v.tensor_scalar(out=s_var[:, t: t + 1], in0=s_var[:, t: t + 1],
                                        scalar1=0.25, scalar2=None, op0=ALU.mult)
                sDVE.inc(i, 1)

            block.vector(ph_ln_stats)

            def ph_ln_rstd(a):
                a.wait_ge(sDVE.h, sDVE.n)
                for t in range(2):
                    a.activation(out=s_r1[:], in_=s_var[:, t: t + 1], func=AF.Ln, bias=s_eps[:])
                    i = a.activation(out=s_rstd[:, t: t + 1], in_=s_r1[:], func=AF.Exp, scale=-0.5)
                sACT.inc(i, 1)

            block.scalar(ph_ln_rstd)

            def ph_ln_nmr(v):
                v.wait_ge(sACT.h, sACT.n)
                for t in range(2):
                    i = v.tensor_scalar(out=s_nmr[:, t: t + 1], in0=s_sm[:, t: t + 1],
                                        scalar1=s_rstd[:, t: t + 1], scalar2=-0.25,
                                        op0=ALU.mult, op1=ALU.mult)
                sDVE.inc(i, 1)

            block.vector(ph_ln_nmr)

            def ph_ln_apply(a):
                a.wait_ge(sDVE.h, sDVE.n)
                for t in range(2):
                    i = a.activation(out=s_xnt[:, t, :], in_=s_xld[:, t, :], func=AF.Identity,
                                     scale=s_rstd[:, t: t + 1], bias=s_nmr[:, t: t + 1])
                sACT.inc(i, 1)

            block.scalar(ph_ln_apply)

            def ph_ln_tr(sy):
                wait_all(sy)
                for t in range(2):
                    i = sy.dma_start_transpose(out=s_xn[:, :, 128 * t: 128 * (t + 1)],
                                               in_=s_xnt[:, t, :])
                    sX.inc(i)

            block.sync(ph_ln_tr)

            for pout in range(16):
                sl = pout % 2
                need_slab_free = rot_slab.free_at[sl]

                def g_slab(g, pout=pout, sl=sl, need=need_slab_free, first=(pout == 0)):
                    if first:
                        wait_all(g)
                    if need:
                        g.wait_ge(sPE.h, need)
                    i = g.dma_start(out=s_wslab[:, sl, :, :],
                                    in_=d_winbf[:, 128 * pout:128 * (pout + 1)].rearrange(
                                        "(kb p) m -> p kb m", p=128))
                    sW.inc(i)

                block.gpsimd(g_slab)
                need_w = sW.n
                need_free = rot_pa.free_at[pout % 2]

                def pe_mm(te, pout=pout, sl=sl, need_w=need_w, need_free=need_free,
                          first=(pout == 0)):
                    if first:
                        wait_all(te)
                    te.wait_ge(sW.h, need_w)
                    if need_free:
                        te.wait_ge(sACT.h, need_free)
                    ps = pa(pout)
                    for k in range(NKB):
                        i = te.matmul(ps, s_wslab[:, sl, k, :], s_xn[:, k, :],
                                      start=(k == 0), stop=(k == NKB - 1))
                    sPE.inc(i, 1)

                block.tensor(pe_mm)
                rot_slab.free_at[sl] = sPE.n
                rot_pa.ready_at[pout % 2] = sPE.n
                need_pe = sPE.n

                def a_evac(a, pout=pout, need_pe=need_pe):
                    a.wait_ge(sPE.h, need_pe)
                    ps = pa(pout)
                    if pout < 8:
                        i = a.activation(out=s_xpre[:, pout, 3:3 + LC], in_=ps, func=AF.Copy)
                    else:
                        i = a.activation(out=s_z[:, pout - 8, :], in_=ps, func=AF.Copy)
                    sACT.inc(i, 1)

                block.scalar(a_evac)
                rot_pa.free_at[pout % 2] = sACT.n

            for d in range(NDB):
                need_free = rot_pa.free_at[d % 2]

                def pe_conv(te, d=d, need_free=need_free):
                    if d == 0:
                        wait_all(te)
                    if need_free:
                        te.wait_ge(sACT.h, need_free)
                    ps = pa(d)
                    for k in range(DC):
                        i = te.matmul(ps, s_convdiag[:, d, k, :],
                                      s_xpre[:, d, k: k + LC],
                                      start=(k == 0), stop=(k == DC - 1))
                    sPE.inc(i, 1)

                block.tensor(pe_conv)
                need_pe = sPE.n

                def a_conv(a, d=d, need_pe=need_pe):
                    a.wait_ge(sPE.h, need_pe)
                    a.activation(out=s_xcf[:, d, :], in_=pa(d), func=AF.Silu,
                                 bias=s_convb[:, d: d + 1])
                    i = a.activation(out=s_xcb[:, d, :], in_=s_xcf[:, d, :], func=AF.Copy)
                    sACT.inc(i, 1)

                block.scalar(a_conv)
                rot_pa.free_at[d % 2] = sACT.n

            def v_halo(v):
                wait_all(v)
                i = v.tensor_copy(s_xpre[:, :, 0:3], s_xpre[:, :, LC:LC + 3])
                sDVE.inc(i, 1)

            block.vector(v_halo)

            def ph_xproj(te):
                wait_all(te)
                for k in range(NDB):
                    te.matmul(P[:, 0, 0:LC], s_xprojT[:, k, 0:128], s_xcb[:, k, :],
                              start=(k == 0), stop=(k == NDB - 1))
                for k in range(NDB):
                    i = te.matmul(P[0:32, 1, 0:LC], s_xprojT[:, k, 128:160], s_xcb[:, k, :],
                                  start=(k == 0), stop=(k == NDB - 1))
                sPE.inc(i, 1)

            block.tensor(ph_xproj)

            def ph_xproj_ev(a):
                a.wait_ge(sPE.h, sPE.n)
                a.activation(out=s_xa[:], in_=P[:, 0, 0:LC], func=AF.Copy)
                i = a.activation(out=s_xb32[:], in_=P[0:32, 1, 0:LC], func=AF.Copy)
                sACT.inc(i, 1)

            block.scalar(ph_xproj_ev)
            rot_pa.free_at[0] = sACT.n
            rot_pa.free_at[1] = sACT.n

            def ph_ar(g, c=c):
                g.wait_ge(sACT.h, sACT.n)
                i = g.dma_start(out=ar_in[c].ap()[0:128, :], in_=s_xa[:])
                sW.inc(i)
                i = g.dma_start(out=ar_in[c].ap()[128:160, :], in_=s_xb32[:])
                sW.inc(i)
                g.wait_ge(sW.h, sW.n)
                i = g.collective_compute("AllReduce", ALU.add, replica_groups=RG,
                                         ins=[ar_in[c].ap().opt()], outs=[ar_out[c].ap().opt()])
                sCC.inc(i, 1)
                g.wait_ge(sCC.h, sCC.n)
                i = g.dma_start(out=s_t1[:], in_=ar_out[c].ap()[0:128, :])
                sW.inc(i)
                src_b = ar_out[c].ap()[128:144, :].rearrange("(o n) l -> o n l", o=1).broadcast_to(
                    [128, DS, LC])
                src_c = ar_out[c].ap()[144:160, :].rearrange("(o n) l -> o n l", o=1).broadcast_to(
                    [128, DS, LC])
                i = g.dma_start(out=s_B[:], in_=src_b)
                sW.inc(i)
                i = g.dma_start(out=s_C[:], in_=src_c)
                sW.inc(i)

            block.gpsimd(ph_ar)

            def ph_dtlow(a):
                a.wait_ge(sW.h, sW.n)
                i = a.activation(out=s_dtlow[:], in_=s_t1[:], func=AF.Copy)
                sACT.inc(i, 1)

            block.scalar(ph_dtlow)

            for d in range(NDB):
                need_free = rot_pa.free_at[d % 2]

                def pe_dt(te, d=d, need_free=need_free):
                    if d == 0:
                        te.wait_ge(sACT.h, sACT.n)
                    if need_free:
                        te.wait_ge(sACT.h, need_free)
                    i = te.matmul(pa(d), s_dtprojT[:, 128 * d:128 * (d + 1)], s_dtlow[:],
                                  start=True, stop=True)
                    sPE.inc(i, 1)

                block.tensor(pe_dt)
                need_pe = sPE.n

                def a_dt(a, d=d, need_pe=need_pe):
                    a.wait_ge(sPE.h, need_pe)
                    ps = pa(d)
                    a.activation(out=s_t1[:], in_=ps, func=AF.Abs, bias=s_dtb2[:, d: d + 1])
                    a.activation(out=s_t3[:], in_=ps, func=AF.Relu, bias=s_dtb2[:, d: d + 1])
                    a.activation(out=s_t2[:], in_=s_t1[:], func=AF.Exp, scale=-1.0)
                    i = a.activation(out=s_t1[:], in_=s_t2[:], func=AF.Ln, bias=s_one[:])
                    sACT.inc(i, 1)

                block.scalar(a_dt)
                rot_pa.free_at[d % 2] = sACT.n
                need_a = sACT.n

                def v_dt(v, d=d, need_a=need_a):
                    v.wait_ge(sACT.h, need_a)
                    i = v.tensor_tensor(out=s_dt[:, d, :], in0=s_t1[:], in1=s_t3[:], op=ALU.add)
                    sDVE.inc(i, 1)

                block.vector(v_dt)

            need_bc = sW.n
            for d in range(NDB):
                for q in range(4):
                    dsl2 = (4 * d + q) % 2
                    need_dA_free = rot_dA.free_at[dsl2]

                    def a_dA(a, d=d, q=q, dsl2=dsl2, need=need_dA_free):
                        if need:
                            a.wait_ge(sDVE.h, need)
                        for n4 in range(4):
                            i = a.activation(out=s_dA[:, dsl2, n4, :], in_=s_dt[:, d, :],
                                             func=AF.Exp,
                                             scale=s_aneg[:, d, 4 * q + n4: 4 * q + n4 + 1])
                        sACT.inc(i, 1)

                    block.scalar(a_dA)
                    rot_dA.ready_at[dsl2] = sACT.n
                    need_a = sACT.n

                    def v_scan(v, d=d, q=q, dsl2=dsl2, need_a=need_a, need_bc=need_bc,
                               first=(d == 0 and q == 0)):
                        if first:
                            wait_all(v)
                            v.wait_ge(sW.h, need_bc)
                        v.wait_ge(sACT.h, need_a)
                        if q == 0:
                            v.tensor_tensor(out=s_dtx[:], in0=s_dt[:, d, :], in1=s_xcf[:, d, :],
                                            op=ALU.mult)
                        dtxv = s_dtx[:].rearrange("p (o l) -> p o l", o=1).broadcast_to([128, 4, LC])
                        v.tensor_tensor(out=s_dBu[:], in0=dtxv, in1=s_B[:, 4 * q:4 * q + 4, :],
                                        op=ALU.mult)
                        for n4 in range(4):
                            n = 4 * q + n4
                            i = v.tensor_tensor_scan(out=s_h[:, n4, 0:LC],
                                                     data0=s_dA[:, dsl2, n4, :],
                                                     data1=s_dBu[:, n4, :],
                                                     initial=s_carry[:, d, n: n + 1],
                                                     op0=ALU.mult, op1=ALU.add)
                        sDVE.inc(i, 1)
                        v.tensor_tensor(out=s_p[:], in0=s_h[:, :, 0:LC], in1=s_C[:, 4 * q:4 * q + 4, :],
                                        op=ALU.mult)
                        pv = s_p[:].rearrange("p n l -> p l n")
                        if q == 0:
                            i = v.tensor_reduce(out=s_yq[:], in_=pv, axis=AX.X, op=ALU.add)
                        else:
                            v.tensor_reduce(out=s_yt[:], in_=pv, axis=AX.X, op=ALU.add)
                            i = v.tensor_tensor(out=s_yq[:], in0=s_yq[:], in1=s_yt[:], op=ALU.add)
                        if q == 3:
                            v.scalar_tensor_tensor(out=s_yt[:], in0=s_xcf[:, d, :],
                                                   scalar=s_dvec[:, d: d + 1], in1=s_yq[:],
                                                   op0=ALU.mult, op1=ALU.add)
                            i = v.tensor_tensor(out=s_yz[:, d, :], in0=s_yt[:], in1=s_z[:, d, :],
                                                op=ALU.mult)
                        sDVE.inc(i, 1)

                    block.vector(v_scan)
                    rot_dA.free_at[dsl2] = sDVE.n
                    need_tts = sDVE.n

                    def a_carry(a, d=d, q=q, need_tts=need_tts):
                        a.wait_ge(sDVE.h, need_tts)
                        i = a.activation(
                            out=s_carry[:, d, 4 * q:4 * q + 4],
                            in_=s_h[:, :, LC - 1:LC].rearrange("p n o -> p (n o)"),
                            func=AF.Copy)
                        sACT.inc(i, 1)

                    block.scalar(a_carry)

            for ib in range(NIB):
                sl = ib % 2
                need_slab = rot_slab.free_at[sl]

                def g_gu(g, ib=ib, sl=sl, need=need_slab, first=(ib == 0)):
                    if first:
                        wait_all(g)
                    if need:
                        g.wait_ge(sPE.h, need)
                    i = g.dma_start(out=s_wslab[:, sl, :, :],
                                    in_=d_gbf[:, 128 * ib:128 * (ib + 1)].rearrange(
                                        "(kb p) m -> p kb m", p=128))
                    sW.inc(i)
                    i = g.dma_start(out=s_wslab2[:, sl, :, :],
                                    in_=d_ubf[:, 128 * ib:128 * (ib + 1)].rearrange(
                                        "(kb p) m -> p kb m", p=128))
                    sW.inc(i)

                block.gpsimd(g_gu)
                need_w = sW.n
                need_gu_free = rot_gu.free_at[ib % 2]

                def pe_gu(te, ib=ib, sl=sl, need_w=need_w, need_gu=need_gu_free,
                          first=(ib == 0)):
                    if first:
                        wait_all(te)
                    te.wait_ge(sW.h, need_w)
                    if need_gu:
                        te.wait_ge(sDVE.h, need_gu)
                    for k in range(NKB):
                        te.matmul(pgk(ib), s_wslab[:, sl, k, :], s_xn[:, k, :],
                                  start=(k == 0), stop=(k == NKB - 1))
                    for k in range(NKB):
                        i = te.matmul(puk(ib), s_wslab2[:, sl, k, :], s_xn[:, k, :],
                                      start=(k == 0), stop=(k == NKB - 1))
                    sPE.inc(i, 1)

                block.tensor(pe_gu)
                rot_slab.free_at[sl] = sPE.n
                need_pe = sPE.n
                need_sg_free = sDVE.n

                def a_silu(a, ib=ib, need_pe=need_pe, need_sg=need_sg_free):
                    a.wait_ge(sPE.h, need_pe)
                    if need_sg:
                        a.wait_ge(sDVE.h, need_sg)
                    i = a.activation(out=s_sg[:], in_=pgk(ib), func=AF.Silu,
                                     scale=s_gsc[:, ib: ib + 1])
                    sACT.inc(i, 1)

                block.scalar(a_silu)
                need_a = sACT.n

                def v_h(v, ib=ib, need_a=need_a):
                    v.wait_ge(sACT.h, need_a)
                    i = v.scalar_tensor_tensor(out=s_hf[:, ib, :], in0=puk(ib),
                                               scalar=s_usc[:, ib: ib + 1], in1=s_sg[:],
                                               op0=ALU.mult, op1=ALU.mult)
                    sDVE.inc(i, 1)

                block.vector(v_h)
                rot_gu.free_at[ib % 2] = sDVE.n

            for pg in range(2):
                for k in range(NIB):
                    sl = k % 2
                    need_dn = rot_dn.free_at[sl]

                    def g_dn(g, pg=pg, k=k, sl=sl, need=need_dn, first=(pg == 0 and k == 0)):
                        if first:
                            wait_all(g)
                        if need:
                            g.wait_ge(sPE.h, need)
                        i = g.dma_start(out=s_dnslab[:, sl, :],
                                        in_=d_dnbf[128 * k:128 * (k + 1),
                                                   1024 * pg:1024 * (pg + 1)])
                        sW.inc(i)

                    block.gpsimd(g_dn)
                    need_w = sW.n
                    need_ev = sACT.n

                    def pe_dn(te, pg=pg, k=k, sl=sl, need_w=need_w, need_ev=need_ev):
                        te.wait_ge(sW.h, need_w)
                        if k == 0:
                            wait_all(te)
                        for pp in range(8):
                            i = te.matmul(pdn(pp), s_dnslab[:, sl, 128 * pp:128 * (pp + 1)],
                                          s_hf[:, k, :], start=(k == 0), stop=(k == NIB - 1))
                        sPE.inc(i, 1)

                    block.tensor(pe_dn)
                    rot_dn.free_at[sl] = sPE.n

                need_pe = sPE.n
                for pp in range(8):
                    pout = 8 * pg + pp
                    esl = pp % 2
                    need_ev_free = rot_ev.free_at[esl]

                    def a_dn(a, pp=pp, pout=pout, esl=esl, need_pe=need_pe, need=need_ev_free):
                        a.wait_ge(sPE.h, need_pe)
                        if need:
                            a.wait_ge(sW.h, need)
                        i = a.activation(out=s_ev[:, esl, :], in_=pdn(pp), func=AF.Copy,
                                         scale=s_dsc[:, pout: pout + 1])
                        sACT.inc(i, 1)

                    block.scalar(a_dn)
                    need_a = sACT.n

                    def g_f(g, pout=pout, esl=esl, c=c, need_a=need_a):
                        g.wait_ge(sACT.h, need_a)
                        dst = rs_in[c].ap().rearrange("(r hb p) t -> p r hb t", r=4, p=128)[
                            :, :, NHB + pout, :]
                        i = g.dma_start(out=dst,
                                        in_=s_ev[:, esl, :].rearrange("p (r t) -> p r t", r=4))
                        sW.inc(i)

                    block.gpsimd(g_f)
                    rot_ev.free_at[esl] = sW.n

            for pout in range(NHB):
                sl = pout % 2
                need_op = rot_op.free_at[sl]

                def g_op(g, pout=pout, sl=sl, need=need_op, first=(pout == 0)):
                    if first:
                        wait_all(g)
                    if need:
                        g.wait_ge(sPE.h, need)
                    i = g.dma_start(out=s_opslab[:, sl, :, :],
                                    in_=d_opbf[:, 128 * pout:128 * (pout + 1)].rearrange(
                                        "(kb p) m -> p kb m", p=128))
                    sW.inc(i)

                block.gpsimd(g_op)
                need_w = sW.n
                need_free = rot_pa.free_at[pout % 2]

                def pe_op(te, pout=pout, sl=sl, need_w=need_w, need_free=need_free,
                          first=(pout == 0)):
                    if first:
                        wait_all(te)
                    te.wait_ge(sW.h, need_w)
                    if need_free:
                        te.wait_ge(sACT.h, need_free)
                    for k in range(NDB):
                        i = te.matmul(pa(pout), s_opslab[:, sl, k, :], s_yz[:, k, :],
                                      start=(k == 0), stop=(k == NDB - 1))
                    sPE.inc(i, 1)

                block.tensor(pe_op)
                rot_op.free_at[sl] = sPE.n
                need_pe = sPE.n
                esl = pout % 2
                need_ev_free = rot_ev.free_at[esl]

                def a_op(a, pout=pout, esl=esl, need_pe=need_pe, need=need_ev_free):
                    a.wait_ge(sPE.h, need_pe)
                    if need:
                        a.wait_ge(sW.h, need)
                    i = a.activation(out=s_ev[:, esl, :], in_=pa(pout), func=AF.Copy)
                    sACT.inc(i, 1)

                block.scalar(a_op)
                rot_pa.free_at[pout % 2] = sACT.n
                need_a = sACT.n

                def g_s(g, pout=pout, esl=esl, c=c, need_a=need_a):
                    g.wait_ge(sACT.h, need_a)
                    dst = rs_in[c].ap().rearrange("(r hb p) t -> p r hb t", r=4, p=128)[
                        :, :, pout, :]
                    i = g.dma_start(out=dst,
                                    in_=s_ev[:, esl, :].rearrange("p (r t) -> p r t", r=4))
                    sW.inc(i)

                block.gpsimd(g_s)
                rot_ev.free_at[esl] = sW.n

            def g_rs(g, c=c):
                g.wait_ge(sW.h, sW.n)
                i = g.collective_compute("ReduceScatter", ALU.add, replica_groups=RG,
                                         ins=[rs_in[c].ap().opt()], outs=[rs_out[c].ap().opt()])
                sCC.inc(i, 1)

            block.gpsimd(g_rs)

            def g_fin(g, c=c):
                g.wait_ge(sCC.h, sCC.n)
                wait_all(g)
                i = g.dma_start(out=s_fs[:], in_=rs_out[c].ap()[0:DM, :].rearrange(
                    "(hb p) t -> p hb t", p=128))
                sW.inc(i)
                i = g.dma_start(out=s_ff[:], in_=rs_out[c].ap()[DM:2 * DM, :].rearrange(
                    "(hb p) t -> p hb t", p=128))
                sW.inc(i)
                i = g.dma_start(out=s_fx[:], in_=d_xfinT[c].rearrange("(hb p) t -> p hb t", p=128))
                sW.inc(i)

            block.gpsimd(g_fin)

            def a_fin(a):
                a.wait_ge(sW.h, sW.n)
                i = a.activation(out=s_fg[:].rearrange("p a b -> p (a b)"),
                                 in_=s_ff[:].rearrange("p a b -> p (a b)"), func=AF.Silu)
                sACT.inc(i, 1)

            block.scalar(a_fin)

            def v_fin(v):
                v.wait_ge(sACT.h, sACT.n)
                v.tensor_tensor(out=s_fo[:], in0=s_fs[:], in1=s_fg[:], op=ALU.mult)
                i = v.tensor_tensor(out=s_fo[:], in0=s_fo[:], in1=s_fx[:], op=ALU.add)
                sDVE.inc(i, 1)

            block.vector(v_fin)

            def sy_out(sy, c=c):
                sy.wait_ge(sDVE.h, sDVE.n)
                i = sy.dma_start(out=d_out[c].rearrange("(hb p) t -> p hb t", p=128), in_=s_fo[:])
                sX.inc(i)

            block.sync(sy_out)

            if DEBUG and c == 0:
                def g_dbg(g):
                    wait_all(g)
                    pairs = [("dbg_xn", s_xn[:]), ("dbg_xpre", s_xpre[:]), ("dbg_z", s_z[:]),
                             ("dbg_xcf", s_xcf[:]), ("dbg_dt", s_dt[:]), ("dbg_yz", s_yz[:]),
                             ("dbg_hf", s_hf[:]), ("dbg_fo", s_fo[:]), ("dbg_carry", s_carry[:]),
                             ("dbg_aneg", s_aneg[:]), ("dbg_xld", s_xld[:]),
                             ("dbg_xnt", s_xnt[:]), ("dbg_rstd", s_rstd[:]),
                             ("dbg_nmr", s_nmr[:]), ("dbg_aggr", s_aggr[:]),
                             ("dbg_st", s_stats[:]), ("dbg_ag4", s_ag4[:]),
                             ("dbg_sm", s_sm[:]), ("dbg_var", s_var[:]),
                             ("dbg_h", s_h[:, :, 0:LC])]
                    for name, ap in pairs:
                        i = g.dma_start(out=dbg[name], in_=ap)
                        sW.inc(i)
                    i = g.dma_start(out=dbg["dbg_ar"], in_=ar_out[0].ap())
                    sW.inc(i)
                    i = g.dma_start(out=dbg["dbg_rs"], in_=rs_out[0].ap())
                    sW.inc(i)
                    g.wait_ge(sW.h, sW.n)

                block.gpsimd(g_dbg)

        def g_end(g):
            g.wait_ge(sX.h, sX.n)
            g.wait_ge(sW.h, sW.n)

        block.gpsimd(g_end)

    return nc


def kernel(**inputs):
    x = np.ascontiguousarray(np.asarray(inputs["L0_attn_hidden_states"], np.float32))
    w_in = np.asarray(inputs["L0_attn_in_proj_w"], np.float32)
    conv_w = np.asarray(inputs["L0_attn_conv_w"], np.float32)
    conv_b = np.asarray(inputs["L0_attn_conv_b"], np.float32)
    xproj = np.asarray(inputs["L0_attn_x_proj_w"], np.float32)
    dtproj = np.asarray(inputs["L0_attn_dt_proj_w"], np.float32)
    dtb = np.asarray(inputs["L0_attn_dt_proj_b"], np.float32)
    alog = np.asarray(inputs["L0_attn_A_log"], np.float32)
    dvec = np.asarray(inputs["L0_attn_D"], np.float32)
    op_w = np.asarray(inputs["L0_attn_out_proj_w"], np.float32)
    gate_w = np.asarray(inputs["L0_ffn_gate_w"], np.int32)
    gate_s = np.asarray(inputs["L0_ffn_gate_scale"], np.float32)
    up_w = np.asarray(inputs["L0_ffn_up_w"], np.int32)
    up_s = np.asarray(inputs["L0_ffn_up_scale"], np.float32)
    dn_w = np.asarray(inputs["L0_ffn_down_w"], np.int32)
    dn_s = np.asarray(inputs["L0_ffn_down_scale"], np.float32)

    def arr128(v):
        return np.ascontiguousarray(v.reshape(NDB, 128).T)

    in_maps = []
    for core in range(8):
        b, j = divmod(core, 4)
        Dj = slice(DSL * j, DSL * (j + 1))
        Ij = slice(ISL * j, ISL * (j + 1))
        xb = x[b]
        cw = conv_w[Dj]
        diag = np.zeros((128, NDB, DC, 128), np.float32)
        for t in range(NDB):
            for k in range(DC):
                np.fill_diagonal(diag[:, t, k, :], cw[128 * t:128 * (t + 1), k])
        xfin = np.stack([
            np.ascontiguousarray(xb[LC * c + TSL * j: LC * c + TSL * (j + 1), :].T)
            for c in range(NCH)])
        in_maps.append({
            "x_b": xb,
            "w_inT": np.ascontiguousarray(
                np.concatenate([w_in[Dj], w_in[DI + DSL * j: DI + DSL * (j + 1)]], axis=0).T),
            "convdiag": diag,
            "convb": arr128(conv_b[Dj]),
            "xprojT": np.ascontiguousarray(xproj[:, Dj].T),
            "dtprojT": np.ascontiguousarray(dtproj[Dj].T),
            "dtb": arr128(dtb[Dj]),
            "alog": np.ascontiguousarray(alog[Dj].reshape(NDB, 128, DS).transpose(1, 0, 2)),
            "dvec": arr128(dvec[Dj]),
            "opT": np.ascontiguousarray(op_w[:, Dj].T),
            "gateT": np.ascontiguousarray(gate_w[Ij].T),
            "gsc": np.ascontiguousarray(gate_s[Ij].reshape(NIB, 128).T),
            "upT": np.ascontiguousarray(up_w[Ij].T),
            "usc": np.ascontiguousarray(up_s[Ij].reshape(NIB, 128).T),
            "dnT": np.ascontiguousarray(dn_w[:, Ij].T),
            "dsc": np.ascontiguousarray(dn_s.reshape(NHB, 128).T),
            "xfinT": xfin,
        })

    nc = build_nc()
    res = run_bass_kernel_spmd(nc, in_maps, core_ids=list(range(8)))
    global _last_results
    _last_results = res.results

    out = np.empty((B, L, DM), np.float32)
    for core in range(8):
        b, j = divmod(core, 4)
        oq = np.asarray(res.results[core]["out_q"])
        for c in range(NCH):
            out[b, LC * c + TSL * j: LC * c + TSL * (j + 1), :] = oq[c].T
    return out


# version 03
# speedup vs baseline: 1.0695x; 1.0695x over previous
import contextlib
import numpy as np
import concourse.bass as bass
import concourse.mybir as mybir
from concourse.bass_utils import run_bass_kernel_spmd

F32 = mybir.dt.float32
BF16 = mybir.dt.bfloat16
I32 = mybir.dt.int32
AF = mybir.ActivationFunctionType
ALU = mybir.AluOpType
AX = mybir.AxisListType

B, L, DM = 2, 4096, 2048
DI, DS, DC, RK = 4096, 16, 4, 128
II = 8192
DSL = DI // 4
ISL = II // 4
NCH = 16
LC = L // NCH
TSL = LC // 4
NKB = DM // 128
NDB = DSL // 128
NIB = ISL // 128
NHB = DM // 128
RG = [[0, 1, 2, 3], [4, 5, 6, 7]]
EPS = 1e-5
DEBUG = False
_last_results = None


class Sem:
    def __init__(self, h):
        self.h = h
        self.n = 0

    def inc(self, instr, k=16):
        instr.then_inc(self.h, k)
        self.n += k
        return self.n


class Rot:

    def __init__(self, n):
        self.free_at = [0] * n
        self.ready_at = [0] * n


def build_nc():
    nc = bass.Bass("TRN2", target_bir_lowering=False, debug=False)

    d_x = nc.dram_tensor("x_b", [L, DM], F32, kind="ExternalInput").ap()
    d_winT = nc.dram_tensor("w_inT", [DM, 2048], F32, kind="ExternalInput").ap()
    d_convdiag = nc.dram_tensor("convdiag", [128, NDB, DC, 128], F32, kind="ExternalInput").ap()
    d_convb = nc.dram_tensor("convb", [128, NDB], F32, kind="ExternalInput").ap()
    d_xprojT = nc.dram_tensor("xprojT", [DSL, 160], F32, kind="ExternalInput").ap()
    d_dtprojT = nc.dram_tensor("dtprojT", [RK, DSL], F32, kind="ExternalInput").ap()
    d_dtb = nc.dram_tensor("dtb", [128, NDB], F32, kind="ExternalInput").ap()
    d_alog = nc.dram_tensor("alog", [128, NDB, DS], F32, kind="ExternalInput").ap()
    d_dvec = nc.dram_tensor("dvec", [128, NDB], F32, kind="ExternalInput").ap()
    d_opT = nc.dram_tensor("opT", [DSL, DM], F32, kind="ExternalInput").ap()
    d_gateT = nc.dram_tensor("gateT", [DM, ISL], I32, kind="ExternalInput").ap()
    d_gsc = nc.dram_tensor("gsc", [128, NIB], F32, kind="ExternalInput").ap()
    d_upT = nc.dram_tensor("upT", [DM, ISL], I32, kind="ExternalInput").ap()
    d_usc = nc.dram_tensor("usc", [128, NIB], F32, kind="ExternalInput").ap()
    d_dnT = nc.dram_tensor("dnT", [ISL, DM], I32, kind="ExternalInput").ap()
    d_dsc = nc.dram_tensor("dsc", [128, NHB], F32, kind="ExternalInput").ap()
    d_xfinT = nc.dram_tensor("xfinT", [NCH, DM, TSL], F32, kind="ExternalInput").ap()

    d_out = nc.dram_tensor("out_q", [NCH, DM, TSL], F32, kind="ExternalOutput").ap()
    dbg = {}
    if DEBUG:
        for name, shape, dt_ in [
            ("dbg_xn", [128, NKB, LC], BF16), ("dbg_xpre", [128, NDB, LC + 3], BF16),
            ("dbg_z", [128, NDB, LC], F32), ("dbg_xcf", [128, NDB, LC], F32),
            ("dbg_dt", [128, NDB, LC], F32), ("dbg_yz", [128, NDB, LC], BF16),
            ("dbg_hf", [128, NIB, LC], BF16), ("dbg_ar", [160, LC], F32),
            ("dbg_rs", [2 * DM, TSL], F32), ("dbg_fo", [128, NHB, TSL], F32),
            ("dbg_carry", [128, NDB, DS], F32), ("dbg_aneg", [128, NDB, DS], F32),
            ("dbg_xld", [128, 2, DM], F32), ("dbg_xnt", [128, 2, DM], BF16),
            ("dbg_st", [128, 4, 6], F32), ("dbg_ag4", [128, 4, 2], F32),
            ("dbg_sm", [128, 2], F32), ("dbg_var", [128, 2], F32),
            ("dbg_h", [128, 4, LC], F32),
            ("dbg_rstd", [128, 2], F32), ("dbg_nmr", [128, 2], F32),
            ("dbg_aggr", [128, 2], F32),
        ]:
            dbg[name] = nc.dram_tensor(name, shape, dt_, kind="ExternalOutput").ap()

    d_winbf = nc.dram_tensor("i_winbf", [16, 128, NKB * 128], BF16, kind="Internal").ap()
    d_opbf = nc.dram_tensor("i_opbf", [16, 128, NDB * 128], BF16, kind="Internal").ap()
    d_gbf = nc.dram_tensor("i_gbf", [NIB, 128, NKB * 128], BF16, kind="Internal").ap()
    d_ubf = nc.dram_tensor("i_ubf", [NIB, 128, NKB * 128], BF16, kind="Internal").ap()
    d_dnbf = nc.dram_tensor("i_dnbf", [NIB, 2, 128, 1024], BF16, kind="Internal").ap()
    ar_in = [nc.dram_tensor(f"cc_ar_i{c}", [160, LC], F32, kind="Internal") for c in range(NCH)]
    ar_out = [nc.dram_tensor(f"cc_ar_o{c}", [160, LC], F32, kind="Internal") for c in range(NCH)]
    rs_in = [nc.dram_tensor(f"cc_rs_i{c}", [4 * 2 * DM, TSL], F32, kind="Internal") for c in range(NCH)]
    rs_out = [nc.dram_tensor(f"cc_rs_o{c}", [2 * DM, TSL], F32, kind="Internal") for c in range(NCH)]

    with contextlib.ExitStack() as ctx:
        E = ctx.enter_context

        s_convdiag = E(nc.sbuf_tensor("s_convdiag", [128, NDB, DC, 128], BF16))
        s_convb = E(nc.sbuf_tensor("s_convb", [128, NDB], F32))
        s_dtb2 = E(nc.sbuf_tensor("s_dtb2", [128, NDB], F32))
        s_dtbr = E(nc.sbuf_tensor("s_dtbr", [128, NDB], F32))
        s_aneg = E(nc.sbuf_tensor("s_aneg", [128, NDB, DS], F32))
        s_dvec = E(nc.sbuf_tensor("s_dvec", [128, NDB], F32))
        s_gsc = E(nc.sbuf_tensor("s_gsc", [128, NIB], F32))
        s_usc = E(nc.sbuf_tensor("s_usc", [128, NIB], F32))
        s_dsc = E(nc.sbuf_tensor("s_dsc", [128, NHB], F32))
        s_one = E(nc.sbuf_tensor("s_one", [128, 1], F32))
        s_eps = E(nc.sbuf_tensor("s_eps", [128, 1], F32))
        s_c128 = E(nc.sbuf_tensor("s_c128", [128, 1], F32))
        s_xprojT = E(nc.sbuf_tensor("s_xprojT", [128, NDB, 160], BF16))
        s_dtprojT = E(nc.sbuf_tensor("s_dtprojT", [128, DSL], BF16))
        s_xld = E(nc.sbuf_tensor("s_xld", [128, 2, DM], F32))
        s_xnt = E(nc.sbuf_tensor("s_xnt", [128, 2, DM], BF16))
        s_stats = E(nc.sbuf_tensor("s_stats", [128, 4, 6], F32))
        s_aggr = E(nc.sbuf_tensor("s_aggr", [128, 2], F32))
        s_ag4 = E(nc.sbuf_tensor("s_ag4", [128, 4, 2], F32))
        s_msq = E(nc.sbuf_tensor("s_msq", [128, 4], F32))
        s_sm = E(nc.sbuf_tensor("s_sm", [128, 2], F32))
        s_var = E(nc.sbuf_tensor("s_var", [128, 2], F32))
        s_w1 = E(nc.sbuf_tensor("s_w1", [128, 1], F32))
        s_dum = E(nc.sbuf_tensor("s_dum", [128, 2, 8], F32))
        s_sp = E(nc.sbuf_tensor("s_sp", [128, NDB, LC], F32))
        s_w2 = E(nc.sbuf_tensor("s_w2", [128, 1], F32))
        s_rstd = E(nc.sbuf_tensor("s_rstd", [128, 2], F32))
        s_mean = E(nc.sbuf_tensor("s_mean", [128, 2], F32))
        s_nmr = E(nc.sbuf_tensor("s_nmr", [128, 2], F32))
        s_r1 = E(nc.sbuf_tensor("s_r1", [128, 1], F32))
        s_xn = E(nc.sbuf_tensor("s_xn", [128, NKB, LC], BF16))
        s_xpre = E(nc.sbuf_tensor("s_xpre", [128, NDB, LC + 3], BF16))
        s_z = E(nc.sbuf_tensor("s_z", [128, NDB, LC], F32))
        s_xcf = E(nc.sbuf_tensor("s_xcf", [128, NDB, LC], F32))
        s_xcb = E(nc.sbuf_tensor("s_xcb", [128, NDB, LC], BF16))
        s_dt = E(nc.sbuf_tensor("s_dt", [128, NDB, LC], F32))
        s_dtlow = E(nc.sbuf_tensor("s_dtlow", [128, LC], BF16))
        s_xa = E(nc.sbuf_tensor("s_xa", [128, LC], F32))
        s_xb32 = E(nc.sbuf_tensor("s_xb32", [32, LC], F32))
        s_t1 = E(nc.sbuf_tensor("s_t1", [128, LC], F32))
        s_t2 = E(nc.sbuf_tensor("s_t2", [128, LC], F32))
        s_t3 = E(nc.sbuf_tensor("s_t3", [128, LC], F32))
        s_B = E(nc.sbuf_tensor("s_B", [128, DS, LC], F32))
        s_C = E(nc.sbuf_tensor("s_C", [128, DS, LC], F32))
        s_dA = E(nc.sbuf_tensor("s_dA", [128, 2, 4, LC], F32))
        s_dBu = E(nc.sbuf_tensor("s_dBu", [128, 4, LC], F32))
        s_h = E(nc.sbuf_tensor("s_h", [128, 4, LC + 16], F32))
        s_p = E(nc.sbuf_tensor("s_p", [128, 4, LC], F32))
        s_carry = E(nc.sbuf_tensor("s_carry", [128, NDB, DS], F32))
        s_dtx = E(nc.sbuf_tensor("s_dtx", [128, LC], F32))
        s_yq = E(nc.sbuf_tensor("s_yq", [128, LC], F32))
        s_yt = E(nc.sbuf_tensor("s_yt", [128, LC], F32))
        s_yz = E(nc.sbuf_tensor("s_yz", [128, NDB, LC], BF16))
        s_hf = E(nc.sbuf_tensor("s_hf", [128, NIB, LC], BF16))
        s_sg = E(nc.sbuf_tensor("s_sg", [128, LC], F32))
        s_wslab = E(nc.sbuf_tensor("s_wslab", [128, 2, NKB, 128], BF16))
        s_wslab2 = E(nc.sbuf_tensor("s_wslab2", [128, 2, NKB, 128], BF16))
        s_dnslab = E(nc.sbuf_tensor("s_dnslab", [128, 2, 1024], BF16))
        s_opslab = E(nc.sbuf_tensor("s_opslab", [128, 2, NDB, 128], BF16))
        s_ev = E(nc.sbuf_tensor("s_ev", [128, 2, LC], F32))
        s_fs = E(nc.sbuf_tensor("s_fs", [128, NHB, TSL], F32))
        s_ff = E(nc.sbuf_tensor("s_ff", [128, NHB, TSL], F32))
        s_fg = E(nc.sbuf_tensor("s_fg", [128, NHB, TSL], F32))
        s_fx = E(nc.sbuf_tensor("s_fx", [128, NHB, TSL], F32))
        s_fo = E(nc.sbuf_tensor("s_fo", [128, NHB, TSL], F32))

        P = E(nc.psum_tensor("p_big", [128, 8, 512], F32))

        def pa(i):
            return P[:, i % 2, 0:LC]

        def pgk(i):
            return P[:, 2 + i % 2, 0:LC]

        def puk(i):
            return P[:, 4 + i % 2, 0:LC]

        def pdn(pp):
            return P[:, pp, 0:LC]

        sX = Sem(E(nc.semaphore("semX")))
        sW = Sem(E(nc.semaphore("semW")))
        sCC = Sem(E(nc.semaphore("semCC")))
        sPE = Sem(E(nc.semaphore("semPE")))
        sACT = Sem(E(nc.semaphore("semACT")))
        sDVE = Sem(E(nc.semaphore("semDVE")))
        ALLS = (sX, sW, sCC, sPE, sACT, sDVE)

        block = E(nc.Block())

        def wait_all(e):
            for s in ALLS:
                if s.n:
                    e.wait_ge(s.h, s.n)

        rot_pa = Rot(2)
        rot_gu = Rot(2)
        rot_slab = Rot(2)
        rot_dn = Rot(2)
        rot_op = Rot(2)
        rot_ev = Rot(2)
        rot_dA = Rot(2)

        def p0_consts(g):
            for dst, src in ((s_convb, d_convb), (s_dtbr, d_dtb), (s_dvec, d_dvec),
                             (s_gsc, d_gsc), (s_usc, d_usc), (s_dsc, d_dsc)):
                i = g.dma_start(out=dst[:], in_=src)
                sW.inc(i)
            i = g.dma_start(out=s_xld[:, 0, 0:NDB * DS], in_=d_alog.rearrange("p a b -> p (a b)"))
            sW.inc(i)

        block.gpsimd(p0_consts)

        def p0_dve(v):
            v.memset(s_one[:], 1.0)
            v.memset(s_eps[:], EPS)
            v.memset(s_c128[:], -128.0)
            v.memset(s_carry[:], 0.0)
            v.memset(s_xpre[:, :, 0:3], 0.0)
            i = v.memset(s_dum[:], 0.0)
            sDVE.inc(i, 1)

        block.vector(p0_dve)

        def p0_act(a):
            wait_all(a)
            a.activation(out=s_dtb2[:], in_=s_dtbr[:], func=AF.Identity, scale=2.0)
            i = a.activation(out=s_xld[:, 1, 0:NDB * DS], in_=s_xld[:, 0, 0:NDB * DS], func=AF.Exp)
            sACT.inc(i, 1)

        block.scalar(p0_act)

        def p0_dve2(v):
            wait_all(v)
            i = v.tensor_scalar_mul(s_aneg[:].rearrange("p a b -> p (a b)"),
                                    s_xld[:, 1, 0:NDB * DS], -1.0)
            sDVE.inc(i, 1)

        block.vector(p0_dve2)

        cast_jobs = [
            (d_winT, d_winbf.rearrange("a b m -> (a b) m"), NKB, 2048, False),
            (d_gateT, d_gbf.rearrange("a b m -> (a b) m"), NKB, ISL, True),
            (d_upT, d_ubf.rearrange("a b m -> (a b) m"), NKB, ISL, True),
            (d_dnT, d_dnbf.rearrange("a c (b2 x) m -> (a c b2) (x m)", x=2), NIB, DM, True),
            (d_opT, d_opbf.rearrange("a (b2 x) m -> (a b2) (x m)", x=2), NDB, DM, False),
        ]
        cast_free = Rot(2)
        cast_nst = Rot(2)
        tcount = [0]
        for (src, dst, ntile, width, is_int) in cast_jobs:
            for t in range(ntile):
                sl = tcount[0] % 2
                tcount[0] += 1
                need_a = cast_free.free_at[sl]

                def g_load(g, src=src, t=t, sl=sl, width=width, is_int=is_int, need_a=need_a,
                           first=(tcount[0] == 1)):
                    if first:
                        wait_all(g)
                    if need_a:
                        g.wait_ge(sACT.h, need_a)
                    dstt = s_xld[:, sl, 0:width].bitcast(I32) if is_int else s_xld[:, sl, 0:width]
                    i = g.dma_start(out=dstt, in_=src[128 * t:128 * (t + 1), :])
                    sW.inc(i)

                block.gpsimd(g_load)
                need_w = sW.n
                need_st = cast_nst.free_at[sl]

                def a_cast(a, sl=sl, width=width, is_int=is_int, need_w=need_w, need_st=need_st):
                    a.wait_ge(sW.h, need_w)
                    if need_st:
                        a.wait_ge(sW.h, need_st)
                    if is_int:
                        i = a.activation(out=s_xnt[:, sl, 0:width], in_=s_xld[:, sl, 0:width].bitcast(I32),
                                         func=AF.Identity, bias=s_c128[:])
                    else:
                        i = a.activation(out=s_xnt[:, sl, 0:width], in_=s_xld[:, sl, 0:width],
                                         func=AF.Copy)
                    sACT.inc(i, 1)

                block.scalar(a_cast)
                cast_free.free_at[sl] = sACT.n
                need_a2 = sACT.n

                def g_store(g, dst=dst, t=t, sl=sl, width=width, need_a2=need_a2):
                    g.wait_ge(sACT.h, need_a2)
                    i = g.dma_start(out=dst[128 * t:128 * (t + 1), :], in_=s_xnt[:, sl, 0:width])
                    sW.inc(i)

                block.gpsimd(g_store)
                cast_nst.free_at[sl] = sW.n

        def p0_res_g(g):
            g.wait_ge(sACT.h, sACT.n)
            i = g.dma_start(out=s_xld[:, 0, 0:NDB * 160].rearrange("p (t m) -> p t m", t=NDB),
                            in_=d_xprojT.rearrange("(t p) m -> p t m", p=128))
            sW.inc(i)
            i = g.dma_start(out=s_xld[:, 1, 0:1024], in_=d_dtprojT)
            sW.inc(i)

        block.gpsimd(p0_res_g)

        def p0_res_a(a):
            a.wait_ge(sW.h, sW.n)
            a.activation(out=s_xprojT[:].rearrange("p t m -> p (t m)"),
                         in_=s_xld[:, 0, 0:NDB * 160], func=AF.Copy)
            i = a.activation(out=s_dtprojT[:], in_=s_xld[:, 1, 0:1024], func=AF.Copy)
            sACT.inc(i, 1)

        block.scalar(p0_res_a)

        def p0_res2_g(g):
            g.wait_ge(sACT.h, sACT.n)
            i = g.dma_start(out=s_xld[:].rearrange("p s m -> p (s m)"),
                            in_=d_convdiag.rearrange("p a b m -> p (a b m)"))
            sW.inc(i)

        block.gpsimd(p0_res2_g)

        def p0_res2_a(a):
            a.wait_ge(sW.h, sW.n)
            i = a.activation(out=s_convdiag[:].rearrange("p a b m -> p (a b m)"),
                             in_=s_xld[:].rearrange("p s m -> p (s m)"), func=AF.Copy)
            sACT.inc(i, 1)

        block.scalar(p0_res2_a)

        for c in range(NCH):
            tok0 = c * LC

            def ph_ln_load(g, tok0=tok0):
                wait_all(g)
                for t in range(2):
                    i = g.dma_start(out=s_xld[:, t, :],
                                    in_=d_x[tok0 + 128 * t: tok0 + 128 * (t + 1), :])
                    sW.inc(i)

            block.gpsimd(ph_ln_load)

            def ph_ln_stats(v):
                wait_all(v)
                for t in range(2):
                    for q in range(4):
                        v.bn_stats(s_stats[:, q, :], s_xld[:, t, 512 * q: 512 * (q + 1)])
                    for q in range(4):
                        i = v.bn_aggr(s_ag4[:, q, :], s_stats[:, q, :])
                    sDVE.inc(i, 1)
                    v.wait_ge(sDVE.h, sDVE.n)
                    mview = s_ag4[:, :, 0:1].rearrange("p a b -> p (a b)")
                    vview = s_ag4[:, :, 1:2].rearrange("p a b -> p (a b)")
                    v.tensor_reduce(out=s_sm[:, t: t + 1], in_=mview, axis=AX.X, op=ALU.add)
                    v.tensor_reduce(out=s_w1[:], in_=vview, axis=AX.X, op=ALU.add)
                    i = v.tensor_tensor(out=s_msq[:], in0=mview, in1=mview, op=ALU.mult)
                    sDVE.inc(i, 1)
                    v.wait_ge(sDVE.h, sDVE.n)
                    v.tensor_reduce(out=s_w2[:], in_=s_msq[:], axis=AX.X, op=ALU.add)
                    i = v.tensor_tensor(out=s_msq[:, 0:1], in0=s_sm[:, t: t + 1],
                                        in1=s_sm[:, t: t + 1], op=ALU.mult)
                    sDVE.inc(i, 1)
                    v.wait_ge(sDVE.h, sDVE.n)
                    i = v.scalar_tensor_tensor(out=s_w1[:], in0=s_msq[:, 0:1], scalar=-0.25,
                                               in1=s_w1[:], op0=ALU.mult, op1=ALU.add)
                    sDVE.inc(i, 1)
                    v.wait_ge(sDVE.h, sDVE.n)
                    i = v.scalar_tensor_tensor(out=s_var[:, t: t + 1], in0=s_w2[:], scalar=1.0,
                                               in1=s_w1[:], op0=ALU.mult, op1=ALU.add)
                    sDVE.inc(i, 1)
                    v.wait_ge(sDVE.h, sDVE.n)
                    i = v.tensor_scalar(out=s_var[:, t: t + 1], in0=s_var[:, t: t + 1],
                                        scalar1=0.25, scalar2=None, op0=ALU.mult)
                sDVE.inc(i, 1)

            block.vector(ph_ln_stats)

            def ph_ln_rstd(a):
                a.wait_ge(sDVE.h, sDVE.n)
                for t in range(2):
                    a.activation(out=s_r1[:], in_=s_var[:, t: t + 1], func=AF.Ln, bias=s_eps[:])
                    i = a.activation(out=s_rstd[:, t: t + 1], in_=s_r1[:], func=AF.Exp, scale=-0.5)
                sACT.inc(i, 1)

            block.scalar(ph_ln_rstd)

            def ph_ln_nmr(v):
                v.wait_ge(sACT.h, sACT.n)
                for t in range(2):
                    i = v.tensor_scalar(out=s_nmr[:, t: t + 1], in0=s_sm[:, t: t + 1],
                                        scalar1=s_rstd[:, t: t + 1], scalar2=-0.25,
                                        op0=ALU.mult, op1=ALU.mult)
                sDVE.inc(i, 1)

            block.vector(ph_ln_nmr)

            def ph_ln_apply(a):
                a.wait_ge(sDVE.h, sDVE.n)
                for t in range(2):
                    i = a.activation(out=s_xnt[:, t, :], in_=s_xld[:, t, :], func=AF.Identity,
                                     scale=s_rstd[:, t: t + 1], bias=s_nmr[:, t: t + 1])
                sACT.inc(i, 1)

            block.scalar(ph_ln_apply)

            def ph_ln_tr(sy):
                wait_all(sy)
                for t in range(2):
                    i = sy.dma_start_transpose(out=s_xn[:, :, 128 * t: 128 * (t + 1)],
                                               in_=s_xnt[:, t, :])
                    sX.inc(i)

            block.sync(ph_ln_tr)

            for pout in range(16):
                sl = pout % 2
                need_slab_free = rot_slab.free_at[sl]

                def g_slab(g, pout=pout, sl=sl, need=need_slab_free, first=(pout == 0)):
                    if first:
                        wait_all(g)
                    if need:
                        g.wait_ge(sPE.h, need)
                    i = g.dma_start(out=s_wslab[:, sl, :, :],
                                    in_=d_winbf[pout].rearrange("p (kb m) -> p kb m", kb=NKB))
                    sW.inc(i)

                block.gpsimd(g_slab)
                need_w = sW.n
                need_free = rot_pa.free_at[pout % 2]

                def pe_mm(te, pout=pout, sl=sl, need_w=need_w, need_free=need_free,
                          first=(pout == 0)):
                    if first:
                        wait_all(te)
                    te.wait_ge(sW.h, need_w)
                    if need_free:
                        te.wait_ge(sACT.h, need_free)
                    ps = pa(pout)
                    for k in range(NKB):
                        i = te.matmul(ps, s_wslab[:, sl, k, :], s_xn[:, k, :],
                                      start=(k == 0), stop=(k == NKB - 1))
                    sPE.inc(i, 1)

                block.tensor(pe_mm)
                rot_slab.free_at[sl] = sPE.n
                rot_pa.ready_at[pout % 2] = sPE.n
                need_pe = sPE.n

                def a_evac(a, pout=pout, need_pe=need_pe):
                    a.wait_ge(sPE.h, need_pe)
                    ps = pa(pout)
                    if pout < 8:
                        i = a.activation(out=s_xpre[:, pout, 3:3 + LC], in_=ps, func=AF.Copy)
                    else:
                        i = a.activation(out=s_z[:, pout - 8, :], in_=ps, func=AF.Copy)
                    sACT.inc(i, 1)

                block.scalar(a_evac)
                rot_pa.free_at[pout % 2] = sACT.n

            for d in range(NDB):
                need_free = rot_pa.free_at[d % 2]

                def pe_conv(te, d=d, need_free=need_free):
                    if d == 0:
                        wait_all(te)
                    if need_free:
                        te.wait_ge(sACT.h, need_free)
                    ps = pa(d)
                    for k in range(DC):
                        i = te.matmul(ps, s_convdiag[:, d, k, :],
                                      s_xpre[:, d, k: k + LC],
                                      start=(k == 0), stop=(k == DC - 1))
                    sPE.inc(i, 1)

                block.tensor(pe_conv)
                need_pe = sPE.n

                def a_conv(a, d=d, need_pe=need_pe):
                    a.wait_ge(sPE.h, need_pe)
                    a.activation(out=s_xcf[:, d, :], in_=pa(d), func=AF.Silu,
                                 bias=s_convb[:, d: d + 1])
                    i = a.activation(out=s_xcb[:, d, :], in_=s_xcf[:, d, :], func=AF.Copy)
                    sACT.inc(i, 1)

                block.scalar(a_conv)
                rot_pa.free_at[d % 2] = sACT.n

            def v_halo(v):
                wait_all(v)
                i = v.tensor_copy(s_xpre[:, :, 0:3], s_xpre[:, :, LC:LC + 3])
                sDVE.inc(i, 1)

            block.vector(v_halo)

            def ph_xproj(te):
                wait_all(te)
                for k in range(NDB):
                    te.matmul(P[:, 0, 0:LC], s_xprojT[:, k, 0:128], s_xcb[:, k, :],
                              start=(k == 0), stop=(k == NDB - 1))
                for k in range(NDB):
                    i = te.matmul(P[0:32, 1, 0:LC], s_xprojT[:, k, 128:160], s_xcb[:, k, :],
                                  start=(k == 0), stop=(k == NDB - 1))
                sPE.inc(i, 1)

            block.tensor(ph_xproj)

            def ph_xproj_ev(a):
                a.wait_ge(sPE.h, sPE.n)
                a.activation(out=s_xa[:], in_=P[:, 0, 0:LC], func=AF.Copy)
                i = a.activation(out=s_xb32[:], in_=P[0:32, 1, 0:LC], func=AF.Copy)
                sACT.inc(i, 1)

            block.scalar(ph_xproj_ev)
            rot_pa.free_at[0] = sACT.n
            rot_pa.free_at[1] = sACT.n

            def ph_ar(g, c=c):
                g.wait_ge(sACT.h, sACT.n)
                i = g.dma_start(out=ar_in[c].ap()[0:128, :], in_=s_xa[:])
                sW.inc(i)
                i = g.dma_start(out=ar_in[c].ap()[128:160, :], in_=s_xb32[:])
                sW.inc(i)
                g.wait_ge(sW.h, sW.n)
                i = g.collective_compute("AllReduce", ALU.add, replica_groups=RG,
                                         ins=[ar_in[c].ap().opt()], outs=[ar_out[c].ap().opt()])
                sCC.inc(i, 1)
                g.wait_ge(sCC.h, sCC.n)
                i = g.dma_start(out=s_t1[:], in_=ar_out[c].ap()[0:128, :])
                sW.inc(i)
                src_b = ar_out[c].ap()[128:144, :].rearrange("(o n) l -> o n l", o=1).broadcast_to(
                    [128, DS, LC])
                src_c = ar_out[c].ap()[144:160, :].rearrange("(o n) l -> o n l", o=1).broadcast_to(
                    [128, DS, LC])
                i = g.dma_start(out=s_B[:], in_=src_b)
                sW.inc(i)
                i = g.dma_start(out=s_C[:], in_=src_c)
                sW.inc(i)

            block.gpsimd(ph_ar)

            def ph_dtlow(a):
                a.wait_ge(sW.h, sW.n)
                i = a.activation(out=s_dtlow[:], in_=s_t1[:], func=AF.Copy)
                sACT.inc(i, 1)

            block.scalar(ph_dtlow)

            DTV = s_dA.rearrange("p a b l -> p (a b) l")
            for d in range(NDB):
                need_free = rot_pa.free_at[d % 2]

                def pe_dt(te, d=d, need_free=need_free):
                    if d == 0:
                        te.wait_ge(sACT.h, sACT.n)
                        wait_all(te)
                    if need_free:
                        te.wait_ge(sACT.h, need_free)
                    i = te.matmul(pa(d), s_dtprojT[:, 128 * d:128 * (d + 1)], s_dtlow[:],
                                  start=True, stop=True)
                    sPE.inc(i, 1)

                block.tensor(pe_dt)
                need_pe = sPE.n

                def a_dtev(a, d=d, need_pe=need_pe):
                    a.wait_ge(sPE.h, need_pe)
                    i = a.activation(out=DTV[:, d, :], in_=pa(d), func=AF.Identity,
                                     bias=s_dtb2[:, d: d + 1])
                    sACT.inc(i, 1)

                block.scalar(a_dtev)
                rot_pa.free_at[d % 2] = sACT.n

            def scr(d):
                return s_dBu[:, d, :] if d < 4 else s_h[:, d - 4, 0:LC]

            def a_softplus(a):
                for d in range(NDB):
                    a.activation(out=s_sp[:, d, :], in_=DTV[:, d, :], func=AF.Relu)
                for d in range(NDB):
                    a.activation(out=scr(d), in_=DTV[:, d, :], func=AF.Abs)
                for d in range(NDB):
                    a.activation(out=DTV[:, d, :], in_=scr(d), func=AF.Exp, scale=-1.0)
                for d in range(NDB):
                    i = a.activation(out=scr(d), in_=DTV[:, d, :], func=AF.Ln, bias=s_one[:])
                sACT.inc(i, 1)

            block.scalar(a_softplus)
            need_a_sp = sACT.n

            def v_dt(v, need_a_sp=need_a_sp):
                v.wait_ge(sACT.h, need_a_sp)
                for d in range(NDB):
                    i = v.tensor_tensor(out=s_dt[:, d, :], in0=s_sp[:, d, :], in1=scr(d),
                                        op=ALU.add)
                sDVE.inc(i, 1)

            block.vector(v_dt)

            need_bc = sW.n
            for d in range(NDB):
                for q in range(4):
                    dsl2 = (4 * d + q) % 2
                    need_dA_free = rot_dA.free_at[dsl2]

                    def a_dA(a, d=d, q=q, dsl2=dsl2, need=need_dA_free):
                        if need:
                            a.wait_ge(sDVE.h, need)
                        for n4 in range(4):
                            i = a.activation(out=s_dA[:, dsl2, n4, :], in_=s_dt[:, d, :],
                                             func=AF.Exp,
                                             scale=s_aneg[:, d, 4 * q + n4: 4 * q + n4 + 1])
                        sACT.inc(i, 1)

                    block.scalar(a_dA)
                    rot_dA.ready_at[dsl2] = sACT.n
                    need_a = sACT.n

                    def v_scan(v, d=d, q=q, dsl2=dsl2, need_a=need_a, need_bc=need_bc,
                               first=(d == 0 and q == 0)):
                        if first:
                            wait_all(v)
                            v.wait_ge(sW.h, need_bc)
                        v.wait_ge(sACT.h, need_a)
                        if q == 0:
                            v.tensor_tensor(out=s_dtx[:], in0=s_dt[:, d, :], in1=s_xcf[:, d, :],
                                            op=ALU.mult)
                        dtxv = s_dtx[:].rearrange("p (o l) -> p o l", o=1).broadcast_to([128, 4, LC])
                        v.tensor_tensor(out=s_dBu[:], in0=dtxv, in1=s_B[:, 4 * q:4 * q + 4, :],
                                        op=ALU.mult)
                        for n4 in range(4):
                            n = 4 * q + n4
                            i = v.tensor_tensor_scan(out=s_h[:, n4, 0:LC],
                                                     data0=s_dA[:, dsl2, n4, :],
                                                     data1=s_dBu[:, n4, :],
                                                     initial=s_carry[:, d, n: n + 1],
                                                     op0=ALU.mult, op1=ALU.add)
                        sDVE.inc(i, 1)
                        v.tensor_tensor(out=s_p[:], in0=s_h[:, :, 0:LC], in1=s_C[:, 4 * q:4 * q + 4, :],
                                        op=ALU.mult)
                        pv = s_p[:].rearrange("p n l -> p l n")
                        if q == 0:
                            i = v.tensor_reduce(out=s_yq[:], in_=pv, axis=AX.X, op=ALU.add)
                        else:
                            v.tensor_reduce(out=s_yt[:], in_=pv, axis=AX.X, op=ALU.add)
                            i = v.tensor_tensor(out=s_yq[:], in0=s_yq[:], in1=s_yt[:], op=ALU.add)
                        if q == 3:
                            v.scalar_tensor_tensor(out=s_yt[:], in0=s_xcf[:, d, :],
                                                   scalar=s_dvec[:, d: d + 1], in1=s_yq[:],
                                                   op0=ALU.mult, op1=ALU.add)
                            i = v.tensor_tensor(out=s_yz[:, d, :], in0=s_yt[:], in1=s_z[:, d, :],
                                                op=ALU.mult)
                        sDVE.inc(i, 1)

                    block.vector(v_scan)
                    rot_dA.free_at[dsl2] = sDVE.n
                    need_tts = sDVE.n

                    def a_carry(a, d=d, q=q, need_tts=need_tts):
                        a.wait_ge(sDVE.h, need_tts)
                        i = a.activation(
                            out=s_carry[:, d, 4 * q:4 * q + 4],
                            in_=s_h[:, :, LC - 1:LC].rearrange("p n o -> p (n o)"),
                            func=AF.Copy)
                        sACT.inc(i, 1)

                    block.scalar(a_carry)

            for ib in range(NIB):
                sl = ib % 2
                need_slab = rot_slab.free_at[sl]

                def g_gu(g, ib=ib, sl=sl, need=need_slab, first=(ib == 0)):
                    if first:
                        wait_all(g)
                    if need:
                        g.wait_ge(sPE.h, need)
                    i = g.dma_start(out=s_wslab[:, sl, :, :],
                                    in_=d_gbf[ib].rearrange("p (kb m) -> p kb m", kb=NKB))
                    sW.inc(i)
                    i = g.dma_start(out=s_wslab2[:, sl, :, :],
                                    in_=d_ubf[ib].rearrange("p (kb m) -> p kb m", kb=NKB))
                    sW.inc(i)

                block.gpsimd(g_gu)
                need_w = sW.n
                need_gu_free = rot_gu.free_at[ib % 2]

                def pe_gu(te, ib=ib, sl=sl, need_w=need_w, need_gu=need_gu_free,
                          first=(ib == 0)):
                    if first:
                        wait_all(te)
                    te.wait_ge(sW.h, need_w)
                    if need_gu:
                        te.wait_ge(sDVE.h, need_gu)
                    for k in range(NKB):
                        te.matmul(pgk(ib), s_wslab[:, sl, k, :], s_xn[:, k, :],
                                  start=(k == 0), stop=(k == NKB - 1))
                    for k in range(NKB):
                        i = te.matmul(puk(ib), s_wslab2[:, sl, k, :], s_xn[:, k, :],
                                      start=(k == 0), stop=(k == NKB - 1))
                    sPE.inc(i, 1)

                block.tensor(pe_gu)
                rot_slab.free_at[sl] = sPE.n
                need_pe = sPE.n
                need_sg_free = sDVE.n

                def a_silu(a, ib=ib, need_pe=need_pe, need_sg=need_sg_free):
                    a.wait_ge(sPE.h, need_pe)
                    if need_sg:
                        a.wait_ge(sDVE.h, need_sg)
                    i = a.activation(out=s_sg[:], in_=pgk(ib), func=AF.Silu,
                                     scale=s_gsc[:, ib: ib + 1])
                    sACT.inc(i, 1)

                block.scalar(a_silu)
                need_a = sACT.n

                def v_h(v, ib=ib, need_a=need_a):
                    v.wait_ge(sACT.h, need_a)
                    i = v.scalar_tensor_tensor(out=s_hf[:, ib, :], in0=puk(ib),
                                               scalar=s_usc[:, ib: ib + 1], in1=s_sg[:],
                                               op0=ALU.mult, op1=ALU.mult)
                    sDVE.inc(i, 1)

                block.vector(v_h)
                rot_gu.free_at[ib % 2] = sDVE.n

            for pg in range(2):
                for k in range(NIB):
                    sl = k % 2
                    need_dn = rot_dn.free_at[sl]

                    def g_dn(g, pg=pg, k=k, sl=sl, need=need_dn, first=(pg == 0 and k == 0)):
                        if first:
                            wait_all(g)
                        if need:
                            g.wait_ge(sPE.h, need)
                        i = g.dma_start(out=s_dnslab[:, sl, :], in_=d_dnbf[k, pg])
                        sW.inc(i)

                    block.gpsimd(g_dn)
                    need_w = sW.n
                    need_ev = sACT.n

                    def pe_dn(te, pg=pg, k=k, sl=sl, need_w=need_w, need_ev=need_ev):
                        te.wait_ge(sW.h, need_w)
                        if k == 0:
                            wait_all(te)
                        for pp in range(8):
                            i = te.matmul(pdn(pp), s_dnslab[:, sl, 128 * pp:128 * (pp + 1)],
                                          s_hf[:, k, :], start=(k == 0), stop=(k == NIB - 1))
                        sPE.inc(i, 1)

                    block.tensor(pe_dn)
                    rot_dn.free_at[sl] = sPE.n

                need_pe = sPE.n
                for pp in range(8):
                    pout = 8 * pg + pp
                    esl = pp % 2
                    need_ev_free = rot_ev.free_at[esl]

                    def a_dn(a, pp=pp, pout=pout, esl=esl, need_pe=need_pe, need=need_ev_free):
                        a.wait_ge(sPE.h, need_pe)
                        if need:
                            a.wait_ge(sW.h, need)
                        i = a.activation(out=s_ev[:, esl, :], in_=pdn(pp), func=AF.Copy,
                                         scale=s_dsc[:, pout: pout + 1])
                        sACT.inc(i, 1)

                    block.scalar(a_dn)
                    need_a = sACT.n

                    def g_f(g, pout=pout, esl=esl, c=c, need_a=need_a):
                        g.wait_ge(sACT.h, need_a)
                        dst = rs_in[c].ap().rearrange("(r hb p) t -> p r hb t", r=4, p=128)[
                            :, :, NHB + pout, :]
                        i = g.dma_start(out=dst,
                                        in_=s_ev[:, esl, :].rearrange("p (r t) -> p r t", r=4))
                        sW.inc(i)

                    block.gpsimd(g_f)
                    rot_ev.free_at[esl] = sW.n

            for pout in range(NHB):
                sl = pout % 2
                need_op = rot_op.free_at[sl]

                def g_op(g, pout=pout, sl=sl, need=need_op, first=(pout == 0)):
                    if first:
                        wait_all(g)
                    if need:
                        g.wait_ge(sPE.h, need)
                    i = g.dma_start(out=s_opslab[:, sl, :, :],
                                    in_=d_opbf[pout].rearrange("p (kb m) -> p kb m", kb=NDB))
                    sW.inc(i)

                block.gpsimd(g_op)
                need_w = sW.n
                need_free = rot_pa.free_at[pout % 2]

                def pe_op(te, pout=pout, sl=sl, need_w=need_w, need_free=need_free,
                          first=(pout == 0)):
                    if first:
                        wait_all(te)
                    te.wait_ge(sW.h, need_w)
                    if need_free:
                        te.wait_ge(sACT.h, need_free)
                    for k in range(NDB):
                        i = te.matmul(pa(pout), s_opslab[:, sl, k, :], s_yz[:, k, :],
                                      start=(k == 0), stop=(k == NDB - 1))
                    sPE.inc(i, 1)

                block.tensor(pe_op)
                rot_op.free_at[sl] = sPE.n
                need_pe = sPE.n
                esl = pout % 2
                need_ev_free = rot_ev.free_at[esl]

                def a_op(a, pout=pout, esl=esl, need_pe=need_pe, need=need_ev_free):
                    a.wait_ge(sPE.h, need_pe)
                    if need:
                        a.wait_ge(sW.h, need)
                    i = a.activation(out=s_ev[:, esl, :], in_=pa(pout), func=AF.Copy)
                    sACT.inc(i, 1)

                block.scalar(a_op)
                rot_pa.free_at[pout % 2] = sACT.n
                need_a = sACT.n

                def g_s(g, pout=pout, esl=esl, c=c, need_a=need_a):
                    g.wait_ge(sACT.h, need_a)
                    dst = rs_in[c].ap().rearrange("(r hb p) t -> p r hb t", r=4, p=128)[
                        :, :, pout, :]
                    i = g.dma_start(out=dst,
                                    in_=s_ev[:, esl, :].rearrange("p (r t) -> p r t", r=4))
                    sW.inc(i)

                block.gpsimd(g_s)
                rot_ev.free_at[esl] = sW.n

            def g_rs(g, c=c):
                g.wait_ge(sW.h, sW.n)
                i = g.collective_compute("ReduceScatter", ALU.add, replica_groups=RG,
                                         ins=[rs_in[c].ap().opt()], outs=[rs_out[c].ap().opt()])
                sCC.inc(i, 1)

            block.gpsimd(g_rs)

            def g_fin(g, c=c):
                g.wait_ge(sCC.h, sCC.n)
                wait_all(g)
                i = g.dma_start(out=s_fs[:], in_=rs_out[c].ap()[0:DM, :].rearrange(
                    "(hb p) t -> p hb t", p=128))
                sW.inc(i)
                i = g.dma_start(out=s_ff[:], in_=rs_out[c].ap()[DM:2 * DM, :].rearrange(
                    "(hb p) t -> p hb t", p=128))
                sW.inc(i)
                i = g.dma_start(out=s_fx[:], in_=d_xfinT[c].rearrange("(hb p) t -> p hb t", p=128))
                sW.inc(i)

            block.gpsimd(g_fin)

            def a_fin(a):
                a.wait_ge(sW.h, sW.n)
                i = a.activation(out=s_fg[:].rearrange("p a b -> p (a b)"),
                                 in_=s_ff[:].rearrange("p a b -> p (a b)"), func=AF.Silu)
                sACT.inc(i, 1)

            block.scalar(a_fin)

            def v_fin(v):
                v.wait_ge(sACT.h, sACT.n)
                v.tensor_tensor(out=s_fo[:], in0=s_fs[:], in1=s_fg[:], op=ALU.mult)
                i = v.tensor_tensor(out=s_fo[:], in0=s_fo[:], in1=s_fx[:], op=ALU.add)
                sDVE.inc(i, 1)

            block.vector(v_fin)

            def sy_out(sy, c=c):
                sy.wait_ge(sDVE.h, sDVE.n)
                i = sy.dma_start(out=d_out[c].rearrange("(hb p) t -> p hb t", p=128), in_=s_fo[:])
                sX.inc(i)

            block.sync(sy_out)

            if DEBUG and c == 0:
                def g_dbg(g):
                    wait_all(g)
                    pairs = [("dbg_xn", s_xn[:]), ("dbg_xpre", s_xpre[:]), ("dbg_z", s_z[:]),
                             ("dbg_xcf", s_xcf[:]), ("dbg_dt", s_dt[:]), ("dbg_yz", s_yz[:]),
                             ("dbg_hf", s_hf[:]), ("dbg_fo", s_fo[:]), ("dbg_carry", s_carry[:]),
                             ("dbg_aneg", s_aneg[:]), ("dbg_xld", s_xld[:]),
                             ("dbg_xnt", s_xnt[:]), ("dbg_rstd", s_rstd[:]),
                             ("dbg_nmr", s_nmr[:]), ("dbg_aggr", s_aggr[:]),
                             ("dbg_st", s_stats[:]), ("dbg_ag4", s_ag4[:]),
                             ("dbg_sm", s_sm[:]), ("dbg_var", s_var[:]),
                             ("dbg_h", s_h[:, :, 0:LC])]
                    for name, ap in pairs:
                        i = g.dma_start(out=dbg[name], in_=ap)
                        sW.inc(i)
                    i = g.dma_start(out=dbg["dbg_ar"], in_=ar_out[0].ap())
                    sW.inc(i)
                    i = g.dma_start(out=dbg["dbg_rs"], in_=rs_out[0].ap())
                    sW.inc(i)
                    g.wait_ge(sW.h, sW.n)

                block.gpsimd(g_dbg)

        def g_end(g):
            g.wait_ge(sX.h, sX.n)
            g.wait_ge(sW.h, sW.n)

        block.gpsimd(g_end)

    return nc


def kernel(**inputs):
    x = np.ascontiguousarray(np.asarray(inputs["L0_attn_hidden_states"], np.float32))
    w_in = np.asarray(inputs["L0_attn_in_proj_w"], np.float32)
    conv_w = np.asarray(inputs["L0_attn_conv_w"], np.float32)
    conv_b = np.asarray(inputs["L0_attn_conv_b"], np.float32)
    xproj = np.asarray(inputs["L0_attn_x_proj_w"], np.float32)
    dtproj = np.asarray(inputs["L0_attn_dt_proj_w"], np.float32)
    dtb = np.asarray(inputs["L0_attn_dt_proj_b"], np.float32)
    alog = np.asarray(inputs["L0_attn_A_log"], np.float32)
    dvec = np.asarray(inputs["L0_attn_D"], np.float32)
    op_w = np.asarray(inputs["L0_attn_out_proj_w"], np.float32)
    gate_w = np.asarray(inputs["L0_ffn_gate_w"], np.int32)
    gate_s = np.asarray(inputs["L0_ffn_gate_scale"], np.float32)
    up_w = np.asarray(inputs["L0_ffn_up_w"], np.int32)
    up_s = np.asarray(inputs["L0_ffn_up_scale"], np.float32)
    dn_w = np.asarray(inputs["L0_ffn_down_w"], np.int32)
    dn_s = np.asarray(inputs["L0_ffn_down_scale"], np.float32)

    def arr128(v):
        return np.ascontiguousarray(v.reshape(NDB, 128).T)

    def slab16(w):
        kb = w.shape[0] // 128
        po = w.shape[1] // 128
        return np.ascontiguousarray(
            w.reshape(kb, 128, po, 128).transpose(2, 1, 0, 3).reshape(w.shape))

    def slabdn(w):
        return np.ascontiguousarray(
            w.reshape(16, 128, 2, 1024).transpose(0, 2, 1, 3).reshape(w.shape))

    in_maps = []
    for core in range(8):
        b, j = divmod(core, 4)
        Dj = slice(DSL * j, DSL * (j + 1))
        Ij = slice(ISL * j, ISL * (j + 1))
        xb = x[b]
        cw = conv_w[Dj]
        diag = np.zeros((128, NDB, DC, 128), np.float32)
        for t in range(NDB):
            for k in range(DC):
                np.fill_diagonal(diag[:, t, k, :], cw[128 * t:128 * (t + 1), k])
        xfin = np.stack([
            np.ascontiguousarray(xb[LC * c + TSL * j: LC * c + TSL * (j + 1), :].T)
            for c in range(NCH)])
        in_maps.append({
            "x_b": xb,
            "w_inT": slab16(
                np.concatenate([w_in[Dj], w_in[DI + DSL * j: DI + DSL * (j + 1)]], axis=0).T),
            "convdiag": diag,
            "convb": arr128(conv_b[Dj]),
            "xprojT": np.ascontiguousarray(xproj[:, Dj].T),
            "dtprojT": np.ascontiguousarray(dtproj[Dj].T),
            "dtb": arr128(dtb[Dj]),
            "alog": np.ascontiguousarray(alog[Dj].reshape(NDB, 128, DS).transpose(1, 0, 2)),
            "dvec": arr128(dvec[Dj]),
            "opT": slab16(np.ascontiguousarray(op_w[:, Dj].T)),
            "gateT": slab16(np.ascontiguousarray(gate_w[Ij].T)),
            "gsc": np.ascontiguousarray(gate_s[Ij].reshape(NIB, 128).T),
            "upT": slab16(np.ascontiguousarray(up_w[Ij].T)),
            "usc": np.ascontiguousarray(up_s[Ij].reshape(NIB, 128).T),
            "dnT": slabdn(np.ascontiguousarray(dn_w[:, Ij].T)),
            "dsc": np.ascontiguousarray(dn_s.reshape(NHB, 128).T),
            "xfinT": xfin,
        })

    nc = build_nc()
    res = run_bass_kernel_spmd(nc, in_maps, core_ids=list(range(8)))
    global _last_results
    _last_results = res.results

    out = np.empty((B, L, DM), np.float32)
    for core in range(8):
        b, j = divmod(core, 4)
        oq = np.asarray(res.results[core]["out_q"])
        for c in range(NCH):
            out[b, LC * c + TSL * j: LC * c + TSL * (j + 1), :] = oq[c].T
    return out


# version 10
# speedup vs baseline: 1.2984x; 1.2140x over previous
import contextlib
import numpy as np
import concourse.bass as bass
import concourse.mybir as mybir
from concourse.bass_utils import run_bass_kernel_spmd

F32 = mybir.dt.float32
BF16 = mybir.dt.bfloat16
I32 = mybir.dt.int32
AF = mybir.ActivationFunctionType
ALU = mybir.AluOpType
AX = mybir.AxisListType

B, L, DM = 2, 4096, 2048
DI, DS, DC, RK = 4096, 16, 4, 128
II = 8192
DSL = DI // 4
ISL = II // 4
NCH = 16
LC = L // NCH
TSL = LC // 4
NKB = DM // 128
NDB = DSL // 128
NIB = ISL // 128
NHB = DM // 128
RG = [[0, 1, 2, 3], [4, 5, 6, 7]]
EPS = 1e-5
DEBUG = False
_last_results = None


class Sem:
    def __init__(self, h):
        self.h = h
        self.n = 0

    def inc(self, instr, k=16):
        instr.then_inc(self.h, k)
        self.n += k
        return self.n


class Rot:

    def __init__(self, n):
        self.free_at = [0] * n
        self.ready_at = [0] * n


def build_nc():
    nc = bass.Bass("TRN2", target_bir_lowering=False, debug=False)

    d_x = nc.dram_tensor("x_b", [L, DM], F32, kind="ExternalInput").ap()
    d_winT = nc.dram_tensor("w_inT", [DM, 2048], F32, kind="ExternalInput").ap()
    d_convdiag = nc.dram_tensor("convdiag", [128, NDB, DC, 128], F32, kind="ExternalInput").ap()
    d_convb = nc.dram_tensor("convb", [128, NDB], F32, kind="ExternalInput").ap()
    d_xprojT = nc.dram_tensor("xprojT", [DSL, 160], F32, kind="ExternalInput").ap()
    d_dtprojT = nc.dram_tensor("dtprojT", [RK, DSL], F32, kind="ExternalInput").ap()
    d_dtb = nc.dram_tensor("dtb", [128, NDB], F32, kind="ExternalInput").ap()
    d_alog = nc.dram_tensor("alog", [128, NDB, DS], F32, kind="ExternalInput").ap()
    d_dvec = nc.dram_tensor("dvec", [128, NDB], F32, kind="ExternalInput").ap()
    d_opT = nc.dram_tensor("opT", [DSL, DM], F32, kind="ExternalInput").ap()
    d_gateT = nc.dram_tensor("gateT", [DM, ISL], I32, kind="ExternalInput").ap()
    d_gsc = nc.dram_tensor("gsc", [128, NIB], F32, kind="ExternalInput").ap()
    d_upT = nc.dram_tensor("upT", [DM, ISL], I32, kind="ExternalInput").ap()
    d_usc = nc.dram_tensor("usc", [128, NIB], F32, kind="ExternalInput").ap()
    d_dnT = nc.dram_tensor("dnT", [ISL, DM], I32, kind="ExternalInput").ap()
    d_dsc = nc.dram_tensor("dsc", [128, NHB], F32, kind="ExternalInput").ap()
    d_xfinT = nc.dram_tensor("xfinT", [NCH, DM, TSL], F32, kind="ExternalInput").ap()

    d_out = nc.dram_tensor("out_q", [NCH, DM, TSL], F32, kind="ExternalOutput").ap()
    dbg = {}
    if DEBUG:
        for name, shape, dt_ in [
            ("dbg_xn", [128, NKB, LC], BF16), ("dbg_xpre", [128, NDB, LC + 3], BF16),
            ("dbg_z", [128, NDB, LC], F32), ("dbg_xcf", [128, NDB, LC], F32),
            ("dbg_dt", [128, NDB, LC], F32), ("dbg_yz", [128, NDB, LC], BF16),
            ("dbg_hf", [128, NIB, LC], BF16), ("dbg_ar", [160, LC], F32),
            ("dbg_rs", [2 * DM, TSL], F32), ("dbg_fo", [128, NHB, TSL], F32),
            ("dbg_carry", [128, NDB, DS], F32), ("dbg_aneg", [128, NDB, DS], F32),
            ("dbg_xld", [128, 2, DM], F32), ("dbg_xnt", [128, 2, DM], BF16),
            ("dbg_st", [128, 4, 6], F32), ("dbg_ag4", [128, 4, 2], F32),
            ("dbg_sm", [128, 2], F32), ("dbg_var", [128, 2], F32),
            ("dbg_h", [128, 4, LC], F32),
            ("dbg_rstd", [128, 2], F32), ("dbg_nmr", [128, 2], F32),
            ("dbg_aggr", [128, 2], F32),
        ]:
            dbg[name] = nc.dram_tensor(name, shape, dt_, kind="ExternalOutput").ap()

    d_winbf = nc.dram_tensor("i_winbf", [16, 128, NKB * 128], BF16, kind="Internal").ap()
    d_opbf = nc.dram_tensor("i_opbf", [16, 128, NDB * 128], BF16, kind="Internal").ap()
    d_gbf = nc.dram_tensor("i_gbf", [NIB, 128, NKB * 128], BF16, kind="Internal").ap()
    d_ubf = nc.dram_tensor("i_ubf", [NIB, 128, NKB * 128], BF16, kind="Internal").ap()
    d_dnbf = nc.dram_tensor("i_dnbf", [NIB, 2, 128, 1024], BF16, kind="Internal").ap()
    ar_in = [nc.dram_tensor(f"cc_ar_i{c}", [160, LC], F32, kind="Internal") for c in range(NCH)]
    ar_out = [nc.dram_tensor(f"cc_ar_o{c}", [160, LC], F32, kind="Internal") for c in range(NCH)]
    rs_in = [nc.dram_tensor(f"cc_rs_i{c}", [4 * 2 * DM, TSL], F32, kind="Internal") for c in range(NCH)]
    rs_out = [nc.dram_tensor(f"cc_rs_o{c}", [2 * DM, TSL], F32, kind="Internal") for c in range(NCH)]

    with contextlib.ExitStack() as ctx:
        E = ctx.enter_context

        s_convdiag = E(nc.sbuf_tensor("s_convdiag", [128, NDB, DC, 128], BF16))
        s_convb = E(nc.sbuf_tensor("s_convb", [128, NDB], F32))
        s_dtb2 = E(nc.sbuf_tensor("s_dtb2", [128, NDB], F32))
        s_dtbr = E(nc.sbuf_tensor("s_dtbr", [128, NDB], F32))
        s_aneg = E(nc.sbuf_tensor("s_aneg", [128, NDB, DS], F32))
        s_dvec = E(nc.sbuf_tensor("s_dvec", [128, NDB], F32))
        s_gsc = E(nc.sbuf_tensor("s_gsc", [128, NIB], F32))
        s_usc = E(nc.sbuf_tensor("s_usc", [128, NIB], F32))
        s_dsc = E(nc.sbuf_tensor("s_dsc", [128, NHB], F32))
        s_one = E(nc.sbuf_tensor("s_one", [128, 1], F32))
        s_eps = E(nc.sbuf_tensor("s_eps", [128, 1], F32))
        s_c128 = E(nc.sbuf_tensor("s_c128", [128, 1], F32))
        s_xprojT = E(nc.sbuf_tensor("s_xprojT", [128, NDB, 160], BF16))
        s_dtprojT = E(nc.sbuf_tensor("s_dtprojT", [128, DSL], BF16))
        s_xld = E(nc.sbuf_tensor("s_xld", [128, 2, DM], F32))
        s_xnt = E(nc.sbuf_tensor("s_xnt", [128, 2, DM], BF16))
        s_stats = E(nc.sbuf_tensor("s_stats", [128, 4, 6], F32))
        s_ag4 = E(nc.sbuf_tensor("s_ag4", [128, 4, 2], F32))
        s_msq = E(nc.sbuf_tensor("s_msq", [128, 4], F32))
        s_sm = E(nc.sbuf_tensor("s_sm", [128, 2], F32))
        s_var = E(nc.sbuf_tensor("s_var", [128, 2], F32))
        s_w1 = E(nc.sbuf_tensor("s_w1", [128, 1], F32))
        s_sp = E(nc.sbuf_tensor("s_sp", [128, NDB, LC], F32))
        s_w2 = E(nc.sbuf_tensor("s_w2", [128, 1], F32))
        s_rstd = E(nc.sbuf_tensor("s_rstd", [128, 2], F32))
        s_nmr = E(nc.sbuf_tensor("s_nmr", [128, 2], F32))
        s_r1 = E(nc.sbuf_tensor("s_r1", [128, 1], F32))
        s_xn = E(nc.sbuf_tensor("s_xn", [128, NKB, LC], BF16))
        s_xpre = E(nc.sbuf_tensor("s_xpre", [128, NDB, LC + 3], BF16))
        s_z = E(nc.sbuf_tensor("s_z", [128, NDB, LC], F32))
        s_xcf = E(nc.sbuf_tensor("s_xcf", [128, NDB, LC], F32))
        s_xcb = E(nc.sbuf_tensor("s_xcb", [128, NDB, LC], BF16))
        s_dt = E(nc.sbuf_tensor("s_dt", [128, NDB, LC], F32))
        s_dtlow = E(nc.sbuf_tensor("s_dtlow", [128, LC], BF16))
        s_xa = E(nc.sbuf_tensor("s_xa", [128, LC], F32))
        s_t1 = E(nc.sbuf_tensor("s_t1", [128, LC], F32))
        s_xb32 = s_t1[0:32, :]
        s_B = E(nc.sbuf_tensor("s_B", [128, DS, LC], F32))
        s_C = E(nc.sbuf_tensor("s_C", [128, DS, LC], F32))
        s_dA = E(nc.sbuf_tensor("s_dA", [128, 2, DS, LC], F32))
        s_dBu = E(nc.sbuf_tensor("s_dBu", [128, 4, LC], F32))
        s_h = E(nc.sbuf_tensor("s_h", [128, 4, LC], F32))
        s_p = E(nc.sbuf_tensor("s_p", [128, 4, LC], F32))
        s_carry = E(nc.sbuf_tensor("s_carry", [128, NDB, DS], F32))
        s_dtx = E(nc.sbuf_tensor("s_dtx", [128, LC], F32))
        s_yq = E(nc.sbuf_tensor("s_yq", [128, LC], F32))
        s_yt = E(nc.sbuf_tensor("s_yt", [128, LC], F32))
        s_yz = E(nc.sbuf_tensor("s_yz", [128, NDB, LC], BF16))
        s_hf = E(nc.sbuf_tensor("s_hf", [128, NIB, LC], BF16))
        s_sg = E(nc.sbuf_tensor("s_sg", [128, LC], F32))
        s_wslab = E(nc.sbuf_tensor("s_wslab", [128, 2, NKB, 128], BF16))
        s_wslab2 = E(nc.sbuf_tensor("s_wslab2", [128, 2, NKB, 128], BF16))
        s_dnslab = E(nc.sbuf_tensor("s_dnslab", [128, 2, 1024], BF16))
        s_opslab = E(nc.sbuf_tensor("s_opslab", [128, 2, NDB, 128], BF16))
        s_ev = E(nc.sbuf_tensor("s_ev", [128, 2, LC], F32))
        _finv = s_xld[:].rearrange("p s m -> p (s m)")

        def _fin_tile(j):
            return _finv[:, 1024 * j:1024 * (j + 1)].rearrange(
                "p (hb t) -> p hb t", hb=NHB)

        s_fs = _fin_tile(0)
        s_ff = _fin_tile(1)
        s_fx = _fin_tile(2)
        s_fg = _fin_tile(3)

        P = E(nc.psum_tensor("p_big", [128, 8, 512], F32))

        def pa(i):
            return P[:, i % 2, 0:LC]

        def pgk(i):
            return P[:, 2 + i % 2, 0:LC]

        def puk(i):
            return P[:, 4 + i % 2, 0:LC]

        def pdn(pp):
            return P[:, pp, 0:LC]

        sX = Sem(E(nc.semaphore("semX")))
        sW = Sem(E(nc.semaphore("semW")))
        sCC = Sem(E(nc.semaphore("semCC")))
        sPE = Sem(E(nc.semaphore("semPE")))
        sACT = Sem(E(nc.semaphore("semACT")))
        sDVE = Sem(E(nc.semaphore("semDVE")))
        sY = Sem(E(nc.semaphore("semY")))
        ALLS = (sX, sW, sCC, sPE, sACT, sDVE, sY)

        block = E(nc.Block())

        def wait_all(e):
            for s in ALLS:
                if s.n:
                    e.wait_ge(s.h, s.n)

        rot_pa = Rot(2)
        rot_gu = Rot(2)
        rot_slab = Rot(2)
        rot_dn = Rot(2)
        rot_op = Rot(2)
        rot_ev = Rot(2)
        rot_dA = Rot(2)

        def p0_consts(g):
            for dst, src in ((s_convb, d_convb), (s_dtbr, d_dtb), (s_dvec, d_dvec),
                             (s_gsc, d_gsc), (s_usc, d_usc), (s_dsc, d_dsc)):
                i = g.dma_start(out=dst[:], in_=src)
                sW.inc(i)
            i = g.dma_start(out=s_xld[:, 0, 0:NDB * DS], in_=d_alog.rearrange("p a b -> p (a b)"))
            sW.inc(i)

        block.gpsimd(p0_consts)

        def p0_dve(v):
            v.memset(s_one[:], 1.0)
            v.memset(s_eps[:], EPS)
            v.memset(s_c128[:], -128.0)
            v.memset(s_carry[:], 0.0)
            i = v.memset(s_xpre[:, :, 0:3], 0.0)
            sDVE.inc(i, 1)

        block.vector(p0_dve)

        def p0_act(a):
            wait_all(a)
            a.activation(out=s_dtb2[:], in_=s_dtbr[:], func=AF.Identity, scale=2.0)
            i = a.activation(out=s_xld[:, 1, 0:NDB * DS], in_=s_xld[:, 0, 0:NDB * DS], func=AF.Exp)
            sACT.inc(i, 1)

        block.scalar(p0_act)

        def p0_dve2(v):
            wait_all(v)
            i = v.tensor_scalar_mul(s_aneg[:].rearrange("p a b -> p (a b)"),
                                    s_xld[:, 1, 0:NDB * DS], -1.0)
            sDVE.inc(i, 1)

        block.vector(p0_dve2)

        cast_jobs = [
            (d_winT, d_winbf.rearrange("a b m -> (a b) m"), NKB, 2048, False),
            (d_gateT, d_gbf.rearrange("a b m -> (a b) m"), NKB, ISL, True),
            (d_upT, d_ubf.rearrange("a b m -> (a b) m"), NKB, ISL, True),
            (d_dnT, d_dnbf.rearrange("a c (b2 x) m -> (a c b2) (x m)", x=2), NIB, DM, True),
            (d_opT, d_opbf.rearrange("a (b2 x) m -> (a b2) (x m)", x=2), NDB, DM, False),
        ]
        cast_free = Rot(2)
        cast_nst = Rot(2)
        tcount = [0]
        for (src, dst, ntile, width, is_int) in cast_jobs:
            for t in range(ntile):
                sl = tcount[0] % 2
                tcount[0] += 1
                need_a = cast_free.free_at[sl]

                def g_load(g, src=src, t=t, sl=sl, width=width, is_int=is_int, need_a=need_a,
                           first=(tcount[0] == 1)):
                    if first:
                        wait_all(g)
                    if need_a:
                        g.wait_ge(sACT.h, need_a)
                    dstt = s_xld[:, sl, 0:width].bitcast(I32) if is_int else s_xld[:, sl, 0:width]
                    i = g.dma_start(out=dstt, in_=src[128 * t:128 * (t + 1), :])
                    sW.inc(i)

                block.gpsimd(g_load)
                need_w = sW.n
                need_st = cast_nst.free_at[sl]

                def a_cast(a, sl=sl, width=width, is_int=is_int, need_w=need_w, need_st=need_st):
                    a.wait_ge(sW.h, need_w)
                    if need_st:
                        a.wait_ge(sW.h, need_st)
                    if is_int:
                        i = a.activation(out=s_xnt[:, sl, 0:width], in_=s_xld[:, sl, 0:width].bitcast(I32),
                                         func=AF.Identity, bias=s_c128[:])
                    else:
                        i = a.activation(out=s_xnt[:, sl, 0:width], in_=s_xld[:, sl, 0:width],
                                         func=AF.Copy)
                    sACT.inc(i, 1)

                block.scalar(a_cast)
                cast_free.free_at[sl] = sACT.n
                need_a2 = sACT.n

                def g_store(g, dst=dst, t=t, sl=sl, width=width, need_a2=need_a2):
                    g.wait_ge(sACT.h, need_a2)
                    i = g.dma_start(out=dst[128 * t:128 * (t + 1), :], in_=s_xnt[:, sl, 0:width])
                    sW.inc(i)

                block.gpsimd(g_store)
                cast_nst.free_at[sl] = sW.n

        def p0_res_g(g):
            g.wait_ge(sACT.h, sACT.n)
            i = g.dma_start(out=s_xld[:, 0, 0:NDB * 160].rearrange("p (t m) -> p t m", t=NDB),
                            in_=d_xprojT.rearrange("(t p) m -> p t m", p=128))
            sW.inc(i)
            i = g.dma_start(out=s_xld[:, 1, 0:1024], in_=d_dtprojT)
            sW.inc(i)

        block.gpsimd(p0_res_g)

        def p0_res_a(a):
            a.wait_ge(sW.h, sW.n)
            a.activation(out=s_xprojT[:].rearrange("p t m -> p (t m)"),
                         in_=s_xld[:, 0, 0:NDB * 160], func=AF.Copy)
            i = a.activation(out=s_dtprojT[:], in_=s_xld[:, 1, 0:1024], func=AF.Copy)
            sACT.inc(i, 1)

        block.scalar(p0_res_a)

        def p0_res2_g(g):
            g.wait_ge(sACT.h, sACT.n)
            i = g.dma_start(out=s_xld[:].rearrange("p s m -> p (s m)"),
                            in_=d_convdiag.rearrange("p a b m -> p (a b m)"))
            sW.inc(i)

        block.gpsimd(p0_res2_g)

        def p0_res2_a(a):
            a.wait_ge(sW.h, sW.n)
            i = a.activation(out=s_convdiag[:].rearrange("p a b m -> p (a b m)"),
                             in_=s_xld[:].rearrange("p s m -> p (s m)"), func=AF.Copy)
            sACT.inc(i, 1)

        block.scalar(p0_res2_a)

        prev_dn_evac = 0
        for c in range(NCH):
            tok0 = c * LC

            def ph_ln_load(g, tok0=tok0):
                wait_all(g)
                for t in range(2):
                    i = g.dma_start(out=s_xld[:, t, :],
                                    in_=d_x[tok0 + 128 * t: tok0 + 128 * (t + 1), :])
                    sW.inc(i)

            block.gpsimd(ph_ln_load)

            def ph_ln_stats(v):
                wait_all(v)
                for t in range(2):
                    for q in range(4):
                        v.bn_stats(s_stats[:, q, :], s_xld[:, t, 512 * q: 512 * (q + 1)])
                    for q in range(4):
                        i = v.bn_aggr(s_ag4[:, q, :], s_stats[:, q, :])
                    sDVE.inc(i, 1)
                    v.wait_ge(sDVE.h, sDVE.n)
                    mview = s_ag4[:, :, 0:1].rearrange("p a b -> p (a b)")
                    vview = s_ag4[:, :, 1:2].rearrange("p a b -> p (a b)")
                    v.tensor_reduce(out=s_sm[:, t: t + 1], in_=mview, axis=AX.X, op=ALU.add)
                    v.tensor_reduce(out=s_w1[:], in_=vview, axis=AX.X, op=ALU.add)
                    i = v.tensor_tensor(out=s_msq[:], in0=mview, in1=mview, op=ALU.mult)
                    sDVE.inc(i, 1)
                    v.wait_ge(sDVE.h, sDVE.n)
                    v.tensor_reduce(out=s_w2[:], in_=s_msq[:], axis=AX.X, op=ALU.add)
                    i = v.tensor_tensor(out=s_msq[:, 0:1], in0=s_sm[:, t: t + 1],
                                        in1=s_sm[:, t: t + 1], op=ALU.mult)
                    sDVE.inc(i, 1)
                    v.wait_ge(sDVE.h, sDVE.n)
                    i = v.scalar_tensor_tensor(out=s_w1[:], in0=s_msq[:, 0:1], scalar=-0.25,
                                               in1=s_w1[:], op0=ALU.mult, op1=ALU.add)
                    sDVE.inc(i, 1)
                    v.wait_ge(sDVE.h, sDVE.n)
                    i = v.scalar_tensor_tensor(out=s_var[:, t: t + 1], in0=s_w2[:], scalar=1.0,
                                               in1=s_w1[:], op0=ALU.mult, op1=ALU.add)
                    sDVE.inc(i, 1)
                    v.wait_ge(sDVE.h, sDVE.n)
                    i = v.tensor_scalar(out=s_var[:, t: t + 1], in0=s_var[:, t: t + 1],
                                        scalar1=0.25, scalar2=None, op0=ALU.mult)
                sDVE.inc(i, 1)

            block.vector(ph_ln_stats)

            def ph_ln_rstd(a):
                a.wait_ge(sDVE.h, sDVE.n)
                for t in range(2):
                    a.activation(out=s_r1[:], in_=s_var[:, t: t + 1], func=AF.Ln, bias=s_eps[:])
                    i = a.activation(out=s_rstd[:, t: t + 1], in_=s_r1[:], func=AF.Exp, scale=-0.5)
                sACT.inc(i, 1)

            block.scalar(ph_ln_rstd)

            def ph_ln_nmr(v):
                v.wait_ge(sACT.h, sACT.n)
                for t in range(2):
                    i = v.tensor_scalar(out=s_nmr[:, t: t + 1], in0=s_sm[:, t: t + 1],
                                        scalar1=s_rstd[:, t: t + 1], scalar2=-0.25,
                                        op0=ALU.mult, op1=ALU.mult)
                sDVE.inc(i, 1)

            block.vector(ph_ln_nmr)

            def ph_ln_apply(a):
                a.wait_ge(sDVE.h, sDVE.n)
                for t in range(2):
                    i = a.activation(out=s_xnt[:, t, :], in_=s_xld[:, t, :], func=AF.Identity,
                                     scale=s_rstd[:, t: t + 1], bias=s_nmr[:, t: t + 1])
                sACT.inc(i, 1)

            block.scalar(ph_ln_apply)

            def ph_ln_tr(sy):
                wait_all(sy)
                for t in range(2):
                    i = sy.dma_start_transpose(out=s_xn[:, :, 128 * t: 128 * (t + 1)],
                                               in_=s_xnt[:, t, :])
                    sX.inc(i)

            block.sync(ph_ln_tr)
            lntr_cnt = sX.n

            for pout in range(16):
                sl = pout % 2
                need_slab_free = rot_slab.free_at[sl]

                def g_slab(g, pout=pout, sl=sl, need=need_slab_free, first=(pout == 0)):
                    if first:
                        wait_all(g)
                    if need:
                        g.wait_ge(sPE.h, need)
                    i = g.dma_start(out=s_wslab[:, sl, :, :],
                                    in_=d_winbf[pout].rearrange("p (kb m) -> p kb m", kb=NKB))
                    sW.inc(i)

                block.gpsimd(g_slab)
                need_w = sW.n
                need_free = rot_pa.free_at[pout % 2]

                def pe_mm(te, pout=pout, sl=sl, need_w=need_w, need_free=need_free,
                          first=(pout == 0)):
                    if first:
                        wait_all(te)
                    te.wait_ge(sW.h, need_w)
                    if need_free:
                        te.wait_ge(sACT.h, need_free)
                    ps = pa(pout)
                    for k in range(NKB):
                        i = te.matmul(ps, s_wslab[:, sl, k, :], s_xn[:, k, :],
                                      start=(k == 0), stop=(k == NKB - 1))
                    sPE.inc(i, 1)

                block.tensor(pe_mm)
                rot_slab.free_at[sl] = sPE.n
                rot_pa.ready_at[pout % 2] = sPE.n
                need_pe = sPE.n

                def a_evac(a, pout=pout, need_pe=need_pe):
                    a.wait_ge(sPE.h, need_pe)
                    ps = pa(pout)
                    if pout < 8:
                        i = a.activation(out=s_xpre[:, pout, 3:3 + LC], in_=ps, func=AF.Copy)
                    else:
                        i = a.activation(out=s_z[:, pout - 8, :], in_=ps, func=AF.Copy)
                    sACT.inc(i, 1)

                block.scalar(a_evac)
                rot_pa.free_at[pout % 2] = sACT.n

            for d in range(NDB):
                need_free = rot_pa.free_at[d % 2]

                def pe_conv(te, d=d, need_free=need_free):
                    if d == 0:
                        wait_all(te)
                    if need_free:
                        te.wait_ge(sACT.h, need_free)
                    ps = pa(d)
                    for k in range(DC):
                        i = te.matmul(ps, s_convdiag[:, d, k, :],
                                      s_xpre[:, d, k: k + LC],
                                      start=(k == 0), stop=(k == DC - 1))
                    sPE.inc(i, 1)

                block.tensor(pe_conv)
                need_pe = sPE.n

                def a_conv(a, d=d, need_pe=need_pe):
                    a.wait_ge(sPE.h, need_pe)
                    a.activation(out=s_xcf[:, d, :], in_=pa(d), func=AF.Silu,
                                 bias=s_convb[:, d: d + 1])
                    i = a.activation(out=s_xcb[:, d, :], in_=s_xcf[:, d, :], func=AF.Copy)
                    sACT.inc(i, 1)

                block.scalar(a_conv)
                rot_pa.free_at[d % 2] = sACT.n

            def v_halo(v):
                wait_all(v)
                i = v.tensor_copy(s_xpre[:, :, 0:3], s_xpre[:, :, LC:LC + 3])
                sDVE.inc(i, 1)

            block.vector(v_halo)

            def ph_xproj(te):
                wait_all(te)
                for k in range(NDB):
                    te.matmul(P[:, 0, 0:LC], s_xprojT[:, k, 0:128], s_xcb[:, k, :],
                              start=(k == 0), stop=(k == NDB - 1))
                for k in range(NDB):
                    i = te.matmul(P[0:32, 1, 0:LC], s_xprojT[:, k, 128:160], s_xcb[:, k, :],
                                  start=(k == 0), stop=(k == NDB - 1))
                sPE.inc(i, 1)

            block.tensor(ph_xproj)

            def ph_xproj_ev(a):
                a.wait_ge(sPE.h, sPE.n)
                a.activation(out=s_xa[:], in_=P[:, 0, 0:LC], func=AF.Copy)
                i = a.activation(out=s_xb32, in_=P[0:32, 1, 0:LC], func=AF.Copy)
                sACT.inc(i, 1)

            block.scalar(ph_xproj_ev)
            rot_pa.free_at[0] = sACT.n
            rot_pa.free_at[1] = sACT.n

            def ph_ar(g, c=c):
                g.wait_ge(sACT.h, sACT.n)
                i = g.dma_start(out=ar_in[c].ap()[0:128, :], in_=s_xa[:])
                sW.inc(i)
                i = g.dma_start(out=ar_in[c].ap()[128:160, :], in_=s_xb32)
                sW.inc(i)
                g.wait_ge(sW.h, sW.n)
                i = g.collective_compute("AllReduce", ALU.add, replica_groups=RG,
                                         ins=[ar_in[c].ap().opt()], outs=[ar_out[c].ap().opt()])
                sCC.inc(i, 1)
                g.wait_ge(sCC.h, sCC.n)
                i = g.dma_start(out=s_t1[:], in_=ar_out[c].ap()[0:128, :])
                sW.inc(i)
                src_b = ar_out[c].ap()[128:144, :].rearrange("(o n) l -> o n l", o=1).broadcast_to(
                    [128, DS, LC])
                src_c = ar_out[c].ap()[144:160, :].rearrange("(o n) l -> o n l", o=1).broadcast_to(
                    [128, DS, LC])
                i = g.dma_start(out=s_B[:], in_=src_b)
                sW.inc(i)
                i = g.dma_start(out=s_C[:], in_=src_c)
                sW.inc(i)

            block.gpsimd(ph_ar)

            need_bc = sW.n

            for ib in range(NIB):
                sl = ib % 2
                need_slab = rot_slab.free_at[sl]

                def g_gu(sy, ib=ib, sl=sl, need=need_slab):
                    if need:
                        sy.wait_ge(sPE.h, need)
                    i = sy.dma_start(out=s_wslab[:, sl, :, :],
                                     in_=d_gbf[ib].rearrange("p (kb m) -> p kb m", kb=NKB))
                    sY.inc(i)
                    i = sy.dma_start(out=s_wslab2[:, sl, :, :],
                                     in_=d_ubf[ib].rearrange("p (kb m) -> p kb m", kb=NKB))
                    sY.inc(i)

                block.sync(g_gu)
                need_w = sY.n
                need_gu_free = rot_gu.free_at[ib % 2]

                def pe_gu(te, ib=ib, sl=sl, need_w=need_w, need_gu=need_gu_free,
                          first=(ib == 0), pde=prev_dn_evac, lntr=lntr_cnt):
                    if first:
                        if pde:
                            te.wait_ge(sACT.h, pde)
                        te.wait_ge(sX.h, lntr)
                    te.wait_ge(sY.h, need_w)
                    if need_gu:
                        te.wait_ge(sDVE.h, need_gu)
                    for k in range(NKB):
                        te.matmul(pgk(ib), s_wslab[:, sl, k, :], s_xn[:, k, :],
                                  start=(k == 0), stop=(k == NKB - 1))
                    for k in range(NKB):
                        i = te.matmul(puk(ib), s_wslab2[:, sl, k, :], s_xn[:, k, :],
                                      start=(k == 0), stop=(k == NKB - 1))
                    sPE.inc(i, 1)

                block.tensor(pe_gu)
                rot_slab.free_at[sl] = sPE.n
                need_pe = sPE.n
                need_sg_free = sDVE.n

                def a_silu(a, ib=ib, need_pe=need_pe, need_sg=need_sg_free):
                    a.wait_ge(sPE.h, need_pe)
                    if need_sg:
                        a.wait_ge(sDVE.h, need_sg)
                    i = a.activation(out=s_sg[:], in_=pgk(ib), func=AF.Silu,
                                     scale=s_gsc[:, ib: ib + 1])
                    sACT.inc(i, 1)

                block.scalar(a_silu)
                need_a = sACT.n

                def v_h(v, ib=ib, need_a=need_a):
                    v.wait_ge(sACT.h, need_a)
                    i = v.scalar_tensor_tensor(out=s_hf[:, ib, :], in0=puk(ib),
                                               scalar=s_usc[:, ib: ib + 1], in1=s_sg[:],
                                               op0=ALU.mult, op1=ALU.mult)
                    sDVE.inc(i, 1)

                block.vector(v_h)
                rot_gu.free_at[ib % 2] = sDVE.n
            vh15 = sDVE.n

            def ph_dtlow(a, need_bc=need_bc):
                a.wait_ge(sW.h, need_bc)
                i = a.activation(out=s_dtlow[:], in_=s_t1[:], func=AF.Copy)
                sACT.inc(i, 1)

            block.scalar(ph_dtlow)

            DTV = s_dA[:, 0, :, :]
            for d in range(NDB):
                need_free = rot_pa.free_at[d % 2]

                def pe_dt(te, d=d, need_free=need_free):
                    if d == 0:
                        te.wait_ge(sACT.h, sACT.n)
                        wait_all(te)
                    if need_free:
                        te.wait_ge(sACT.h, need_free)
                    i = te.matmul(pa(d), s_dtprojT[:, 128 * d:128 * (d + 1)], s_dtlow[:],
                                  start=True, stop=True)
                    sPE.inc(i, 1)

                block.tensor(pe_dt)
                need_pe = sPE.n

                def a_dtev(a, d=d, need_pe=need_pe):
                    a.wait_ge(sPE.h, need_pe)
                    i = a.activation(out=DTV[:, d, :], in_=pa(d), func=AF.Identity,
                                     bias=s_dtb2[:, d: d + 1])
                    sACT.inc(i, 1)

                block.scalar(a_dtev)
                rot_pa.free_at[d % 2] = sACT.n
            dtev_last = sACT.n

            def scr(d):
                return s_dBu[:, d, :] if d < 4 else s_h[:, d - 4, 0:LC]

            def a_softplus(a):
                for d in range(NDB):
                    a.activation(out=s_sp[:, d, :], in_=DTV[:, d, :], func=AF.Relu)
                for d in range(NDB):
                    a.activation(out=scr(d), in_=DTV[:, d, :], func=AF.Abs)
                for d in range(NDB):
                    a.activation(out=DTV[:, d, :], in_=scr(d), func=AF.Exp, scale=-1.0)
                for d in range(NDB):
                    i = a.activation(out=scr(d), in_=DTV[:, d, :], func=AF.Ln, bias=s_one[:])
                sACT.inc(i, 1)

            block.scalar(a_softplus)
            need_a_sp = sACT.n

            def v_dt(v, need_a_sp=need_a_sp):
                v.wait_ge(sACT.h, need_a_sp)
                for d in range(NDB):
                    i = v.tensor_tensor(out=s_dt[:, d, :], in0=s_sp[:, d, :], in1=scr(d),
                                        op=ALU.add)
                sDVE.inc(i, 1)

            block.vector(v_dt)
            v_dt_mark = sDVE.n

            dn_evac_cnt = [0] * NHB

            def emit_dn_mms(pg, need_ev):
                for k in range(NIB):
                    sl = k % 2
                    need_dn = rot_dn.free_at[sl]

                    def g_dn(sy, pg=pg, k=k, sl=sl, need=need_dn):
                        if need:
                            sy.wait_ge(sPE.h, need)
                        i = sy.dma_start(out=s_dnslab[:, sl, :], in_=d_dnbf[k, pg])
                        sY.inc(i)

                    block.sync(g_dn)
                    need_w = sY.n

                    def pe_dn(te, pg=pg, k=k, sl=sl, need_w=need_w, need_ev=need_ev,
                              vh=vh15, dte=dtev_last):
                        te.wait_ge(sY.h, need_w)
                        if k == 0:
                            te.wait_ge(sDVE.h, vh)
                            te.wait_ge(sACT.h, dte if pg == 0 else need_ev)
                        for pp in range(8):
                            i = te.matmul(pdn(pp), s_dnslab[:, sl, 128 * pp:128 * (pp + 1)],
                                          s_hf[:, k, :], start=(k == 0), stop=(k == NIB - 1))
                        sPE.inc(i, 1)

                    block.tensor(pe_dn)
                    rot_dn.free_at[sl] = sPE.n
                return sPE.n

            def emit_dn_evacs(pg, need_pe, c=c):
                for pp in range(8):
                    pout = 8 * pg + pp
                    esl = pout % 2
                    need_ev_free = rot_ev.free_at[esl]

                    def a_dn(a, pp=pp, pout=pout, esl=esl, need_pe=need_pe,
                             need=need_ev_free):
                        a.wait_ge(sPE.h, need_pe)
                        if need:
                            a.wait_ge(sW.h, need)
                        i = a.activation(out=s_ev[:, esl, :], in_=pdn(pp), func=AF.Copy,
                                         scale=s_dsc[:, pout: pout + 1])
                        sACT.inc(i, 1)

                    block.scalar(a_dn)
                    dn_evac_cnt[pout] = sACT.n

                    def g_f(g, pout=pout, esl=esl, c=c, need_a=sACT.n):
                        g.wait_ge(sACT.h, need_a)
                        dst = rs_in[c].ap().rearrange("(r hb p) t -> p r hb t", r=4, p=128)[
                            :, :, NHB + pout, :]
                        i = g.dma_start(out=dst,
                                        in_=s_ev[:, esl, :].rearrange("p (r t) -> p r t", r=4))
                        sW.inc(i)

                    block.gpsimd(g_f)
                    rot_ev.free_at[esl] = sW.n
                return sACT.n

            def emit_scan_d(d):
                sl = d % 2
                need_dA_free = max(rot_dA.free_at[sl], v_dt_mark)

                def a_dA(a, d=d, sl=sl, need=need_dA_free):
                    a.wait_ge(sDVE.h, need)
                    for n in range(DS):
                        i = a.activation(out=s_dA[:, sl, n, :], in_=s_dt[:, d, :],
                                         func=AF.Exp,
                                         scale=s_aneg[:, d, n: n + 1])
                    sACT.inc(i, 1)

                block.scalar(a_dA)
                need_a = sACT.n

                def v_scan(v, d=d, sl=sl, need_a=need_a, need_bc=need_bc, first=(d == 0)):
                    v.wait_ge(sACT.h, need_a)
                    if first:
                        v.wait_ge(sW.h, need_bc)
                    v.tensor_tensor(out=s_dtx[:], in0=s_dt[:, d, :], in1=s_xcf[:, d, :],
                                    op=ALU.mult)
                    dtxv = s_dtx[:].rearrange("p (o l) -> p o l", o=1).broadcast_to([128, 4, LC])
                    for q in range(4):
                        v.tensor_tensor(out=s_dBu[:], in0=dtxv, in1=s_B[:, 4 * q:4 * q + 4, :],
                                        op=ALU.mult)
                        for n4 in range(4):
                            n = 4 * q + n4
                            v.tensor_tensor_scan(out=s_h[:, n4, 0:LC],
                                                 data0=s_dA[:, sl, n, :],
                                                 data1=s_dBu[:, n4, :],
                                                 initial=s_carry[:, d, n: n + 1],
                                                 op0=ALU.mult, op1=ALU.add)
                        v.tensor_copy(s_carry[:, d, 4 * q:4 * q + 4],
                                      s_h[:, :, LC - 1:LC].rearrange("p n o -> p (n o)"))
                        v.tensor_tensor(out=s_p[:], in0=s_h[:, :, 0:LC],
                                        in1=s_C[:, 4 * q:4 * q + 4, :], op=ALU.mult)
                        pv = s_p[:].rearrange("p n l -> p l n")
                        if q == 0:
                            v.tensor_reduce(out=s_yq[:], in_=pv, axis=AX.X, op=ALU.add)
                        else:
                            v.tensor_reduce(out=s_yt[:], in_=pv, axis=AX.X, op=ALU.add)
                            v.tensor_tensor(out=s_yq[:], in0=s_yq[:], in1=s_yt[:], op=ALU.add)
                    v.scalar_tensor_tensor(out=s_yt[:], in0=s_xcf[:, d, :],
                                           scalar=s_dvec[:, d: d + 1], in1=s_yq[:],
                                           op0=ALU.mult, op1=ALU.add)
                    i = v.tensor_tensor(out=s_yz[:, d, :], in0=s_yt[:], in1=s_z[:, d, :],
                                        op=ALU.mult)
                    sDVE.inc(i, 1)

                block.vector(v_scan)
                rot_dA.free_at[sl] = sDVE.n

            pg0_done = emit_dn_mms(0, 0)
            for d in range(4):
                emit_scan_d(d)
            pg0_evac_last = emit_dn_evacs(0, pg0_done)
            pg1_done = emit_dn_mms(1, pg0_evac_last)
            for d in range(4, 8):
                emit_scan_d(d)
            scan_done = sDVE.n
            pg1_evac_last = emit_dn_evacs(1, pg1_done)
            prev_dn_evac = pg1_evac_last

            for pout in range(NHB):
                sl = pout % 2
                need_op = rot_op.free_at[sl]

                def g_op(g, pout=pout, sl=sl, need=need_op, first=(pout == 0)):
                    if first:
                        wait_all(g)
                    if need:
                        g.wait_ge(sPE.h, need)
                    i = g.dma_start(out=s_opslab[:, sl, :, :],
                                    in_=d_opbf[pout].rearrange("p (kb m) -> p kb m", kb=NDB))
                    sW.inc(i)

                block.gpsimd(g_op)
                need_w = sW.n
                need_free = rot_pa.free_at[pout % 2]

                def pe_op(te, pout=pout, sl=sl, need_w=need_w, need_free=need_free,
                          first=(pout == 0), sd=scan_done, pge=pg1_evac_last):
                    if first:
                        te.wait_ge(sDVE.h, sd)
                        te.wait_ge(sACT.h, pge)
                    te.wait_ge(sW.h, need_w)
                    if need_free:
                        te.wait_ge(sACT.h, need_free)
                    for k in range(NDB):
                        i = te.matmul(pa(pout), s_opslab[:, sl, k, :], s_yz[:, k, :],
                                      start=(k == 0), stop=(k == NDB - 1))
                    sPE.inc(i, 1)

                block.tensor(pe_op)
                rot_op.free_at[sl] = sPE.n
                need_pe = sPE.n
                esl = pout % 2
                need_ev_free = rot_ev.free_at[esl]

                def a_op(a, pout=pout, esl=esl, need_pe=need_pe, need=need_ev_free):
                    a.wait_ge(sPE.h, need_pe)
                    if need:
                        a.wait_ge(sW.h, need)
                    i = a.activation(out=s_ev[:, esl, :], in_=pa(pout), func=AF.Copy)
                    sACT.inc(i, 1)

                block.scalar(a_op)
                rot_pa.free_at[pout % 2] = sACT.n
                need_a = sACT.n

                def g_s(g, pout=pout, esl=esl, c=c, need_a=need_a):
                    g.wait_ge(sACT.h, need_a)
                    dst = rs_in[c].ap().rearrange("(r hb p) t -> p r hb t", r=4, p=128)[
                        :, :, pout, :]
                    i = g.dma_start(out=dst,
                                    in_=s_ev[:, esl, :].rearrange("p (r t) -> p r t", r=4))
                    sW.inc(i)

                block.gpsimd(g_s)
                rot_ev.free_at[esl] = sW.n

            def g_rs(g, c=c):
                g.wait_ge(sW.h, sW.n)
                i = g.collective_compute("ReduceScatter", ALU.add, replica_groups=RG,
                                         ins=[rs_in[c].ap().opt()], outs=[rs_out[c].ap().opt()])
                sCC.inc(i, 1)

            block.gpsimd(g_rs)

            def g_fin(g, c=c):
                g.wait_ge(sCC.h, sCC.n)
                wait_all(g)
                i = g.dma_start(out=s_fs, in_=rs_out[c].ap()[0:DM, :].rearrange(
                    "(hb p) t -> p hb t", p=128))
                sW.inc(i)
                i = g.dma_start(out=s_ff, in_=rs_out[c].ap()[DM:2 * DM, :].rearrange(
                    "(hb p) t -> p hb t", p=128))
                sW.inc(i)
                i = g.dma_start(out=s_fx, in_=d_xfinT[c].rearrange("(hb p) t -> p hb t", p=128))
                sW.inc(i)

            block.gpsimd(g_fin)

            def a_fin(a):
                a.wait_ge(sW.h, sW.n)
                i = a.activation(out=s_fg.rearrange("p a b -> p (a b)"),
                                 in_=s_ff.rearrange("p a b -> p (a b)"), func=AF.Silu)
                sACT.inc(i, 1)

            block.scalar(a_fin)

            def v_fin(v):
                v.wait_ge(sACT.h, sACT.n)
                v.tensor_tensor(out=s_ff, in0=s_fs, in1=s_fg, op=ALU.mult)
                i = v.tensor_tensor(out=s_fs, in0=s_ff, in1=s_fx, op=ALU.add)
                sDVE.inc(i, 1)

            block.vector(v_fin)

            def sy_out(sy, c=c):
                sy.wait_ge(sDVE.h, sDVE.n)
                i = sy.dma_start(out=d_out[c].rearrange("(hb p) t -> p hb t", p=128), in_=s_fs)
                sX.inc(i)

            block.sync(sy_out)

            if DEBUG and c == 0:
                def g_dbg(g):
                    wait_all(g)
                    pairs = [("dbg_xn", s_xn[:]), ("dbg_xpre", s_xpre[:]), ("dbg_z", s_z[:]),
                             ("dbg_xcf", s_xcf[:]), ("dbg_dt", s_dt[:]), ("dbg_yz", s_yz[:]),
                             ("dbg_hf", s_hf[:]), ("dbg_fo", s_fs), ("dbg_carry", s_carry[:]),
                             ("dbg_aneg", s_aneg[:]), ("dbg_xld", s_xld[:]),
                             ("dbg_xnt", s_xnt[:]), ("dbg_rstd", s_rstd[:]),
                             ("dbg_nmr", s_nmr[:]),
                             ("dbg_st", s_stats[:]), ("dbg_ag4", s_ag4[:]),
                             ("dbg_sm", s_sm[:]), ("dbg_var", s_var[:]),
                             ("dbg_h", s_h[:, :, 0:LC])]
                    for name, ap in pairs:
                        i = g.dma_start(out=dbg[name], in_=ap)
                        sW.inc(i)
                    i = g.dma_start(out=dbg["dbg_ar"], in_=ar_out[0].ap())
                    sW.inc(i)
                    i = g.dma_start(out=dbg["dbg_rs"], in_=rs_out[0].ap())
                    sW.inc(i)
                    g.wait_ge(sW.h, sW.n)

                block.gpsimd(g_dbg)

        def g_end(g):
            g.wait_ge(sX.h, sX.n)
            g.wait_ge(sW.h, sW.n)

        block.gpsimd(g_end)

    return nc


def kernel(**inputs):
    x = np.ascontiguousarray(np.asarray(inputs["L0_attn_hidden_states"], np.float32))
    w_in = np.asarray(inputs["L0_attn_in_proj_w"], np.float32)
    conv_w = np.asarray(inputs["L0_attn_conv_w"], np.float32)
    conv_b = np.asarray(inputs["L0_attn_conv_b"], np.float32)
    xproj = np.asarray(inputs["L0_attn_x_proj_w"], np.float32)
    dtproj = np.asarray(inputs["L0_attn_dt_proj_w"], np.float32)
    dtb = np.asarray(inputs["L0_attn_dt_proj_b"], np.float32)
    alog = np.asarray(inputs["L0_attn_A_log"], np.float32)
    dvec = np.asarray(inputs["L0_attn_D"], np.float32)
    op_w = np.asarray(inputs["L0_attn_out_proj_w"], np.float32)
    gate_w = np.asarray(inputs["L0_ffn_gate_w"], np.int32)
    gate_s = np.asarray(inputs["L0_ffn_gate_scale"], np.float32)
    up_w = np.asarray(inputs["L0_ffn_up_w"], np.int32)
    up_s = np.asarray(inputs["L0_ffn_up_scale"], np.float32)
    dn_w = np.asarray(inputs["L0_ffn_down_w"], np.int32)
    dn_s = np.asarray(inputs["L0_ffn_down_scale"], np.float32)

    def arr128(v):
        return np.ascontiguousarray(v.reshape(NDB, 128).T)

    def slab16(w):
        kb = w.shape[0] // 128
        po = w.shape[1] // 128
        return np.ascontiguousarray(
            w.reshape(kb, 128, po, 128).transpose(2, 1, 0, 3).reshape(w.shape))

    def slabdn(w):
        return np.ascontiguousarray(
            w.reshape(16, 128, 2, 1024).transpose(0, 2, 1, 3).reshape(w.shape))

    in_maps = []
    for core in range(8):
        b, j = divmod(core, 4)
        Dj = slice(DSL * j, DSL * (j + 1))
        Ij = slice(ISL * j, ISL * (j + 1))
        xb = x[b]
        cw = conv_w[Dj]
        diag = np.zeros((128, NDB, DC, 128), np.float32)
        for t in range(NDB):
            for k in range(DC):
                np.fill_diagonal(diag[:, t, k, :], cw[128 * t:128 * (t + 1), k])
        xfin = np.stack([
            np.ascontiguousarray(xb[LC * c + TSL * j: LC * c + TSL * (j + 1), :].T)
            for c in range(NCH)])
        in_maps.append({
            "x_b": xb,
            "w_inT": slab16(
                np.concatenate([w_in[Dj], w_in[DI + DSL * j: DI + DSL * (j + 1)]], axis=0).T),
            "convdiag": diag,
            "convb": arr128(conv_b[Dj]),
            "xprojT": np.ascontiguousarray(xproj[:, Dj].T),
            "dtprojT": np.ascontiguousarray(dtproj[Dj].T),
            "dtb": arr128(dtb[Dj]),
            "alog": np.ascontiguousarray(alog[Dj].reshape(NDB, 128, DS).transpose(1, 0, 2)),
            "dvec": arr128(dvec[Dj]),
            "opT": slab16(np.ascontiguousarray(op_w[:, Dj].T)),
            "gateT": slab16(np.ascontiguousarray(gate_w[Ij].T)),
            "gsc": np.ascontiguousarray(gate_s[Ij].reshape(NIB, 128).T),
            "upT": slab16(np.ascontiguousarray(up_w[Ij].T)),
            "usc": np.ascontiguousarray(up_s[Ij].reshape(NIB, 128).T),
            "dnT": slabdn(np.ascontiguousarray(dn_w[:, Ij].T)),
            "dsc": np.ascontiguousarray(dn_s.reshape(NHB, 128).T),
            "xfinT": xfin,
        })

    nc = build_nc()
    res = run_bass_kernel_spmd(nc, in_maps, core_ids=list(range(8)))
    global _last_results
    _last_results = res.results

    out = np.empty((B, L, DM), np.float32)
    for core in range(8):
        b, j = divmod(core, 4)
        oq = np.asarray(res.results[core]["out_q"])
        for c in range(NCH):
            out[b, LC * c + TSL * j: LC * c + TSL * (j + 1), :] = oq[c].T
    return out


# version 12
# speedup vs baseline: 1.6658x; 1.2830x over previous
import contextlib
import numpy as np
import concourse.bass as bass
import concourse.mybir as mybir
from concourse.bass_utils import run_bass_kernel_spmd

F32 = mybir.dt.float32
BF16 = mybir.dt.bfloat16
I32 = mybir.dt.int32
AF = mybir.ActivationFunctionType
ALU = mybir.AluOpType
AX = mybir.AxisListType

B, L, DM = 2, 4096, 2048
DI, DS, DC, RK = 4096, 16, 4, 128
II = 8192
DSL = DI // 4
ISL = II // 4
NCH = 16
LC = L // NCH
TSL = LC // 4
NKB = DM // 128
NDB = DSL // 128
NIB = ISL // 128
NHB = DM // 128
RG = [[0, 1, 2, 3], [4, 5, 6, 7]]
EPS = 1e-5
DEBUG = False
_last_results = None


class Sem:
    def __init__(self, h):
        self.h = h
        self.n = 0

    def inc(self, instr, k=16):
        instr.then_inc(self.h, k)
        self.n += k
        return self.n


class Rot:

    def __init__(self, n):
        self.free_at = [0] * n
        self.ready_at = [0] * n


def build_nc():
    nc = bass.Bass("TRN2", target_bir_lowering=False, debug=False)

    d_x = nc.dram_tensor("x_b", [L, DM], F32, kind="ExternalInput").ap()
    d_winT = nc.dram_tensor("w_inT", [DM, 2048], F32, kind="ExternalInput").ap()
    d_convdiag = nc.dram_tensor("convdiag", [128, NDB, DC, 128], F32, kind="ExternalInput").ap()
    d_convb = nc.dram_tensor("convb", [128, NDB], F32, kind="ExternalInput").ap()
    d_xprojT = nc.dram_tensor("xprojT", [DSL, 160], F32, kind="ExternalInput").ap()
    d_dtprojT = nc.dram_tensor("dtprojT", [RK, DSL], F32, kind="ExternalInput").ap()
    d_dtb = nc.dram_tensor("dtb", [128, NDB], F32, kind="ExternalInput").ap()
    d_alog = nc.dram_tensor("alog", [128, NDB, DS], F32, kind="ExternalInput").ap()
    d_dvec = nc.dram_tensor("dvec", [128, NDB], F32, kind="ExternalInput").ap()
    d_opT = nc.dram_tensor("opT", [DSL, DM], F32, kind="ExternalInput").ap()
    d_gateT = nc.dram_tensor("gateT", [DM, ISL], I32, kind="ExternalInput").ap()
    d_gsc = nc.dram_tensor("gsc", [128, NIB], F32, kind="ExternalInput").ap()
    d_upT = nc.dram_tensor("upT", [DM, ISL], I32, kind="ExternalInput").ap()
    d_usc = nc.dram_tensor("usc", [128, NIB], F32, kind="ExternalInput").ap()
    d_dnT = nc.dram_tensor("dnT", [ISL, DM], I32, kind="ExternalInput").ap()
    d_dsc = nc.dram_tensor("dsc", [128, NHB], F32, kind="ExternalInput").ap()
    d_xfinT = nc.dram_tensor("xfinT", [NCH, DM, TSL], F32, kind="ExternalInput").ap()

    d_out = nc.dram_tensor("out_q", [NCH, DM, TSL], F32, kind="ExternalOutput").ap()
    dbg = {}
    if DEBUG:
        for name, shape, dt_ in [
            ("dbg_xn", [128, NKB, LC], BF16), ("dbg_xpre", [128, NDB, LC + 3], BF16),
            ("dbg_z", [128, NDB, LC], F32), ("dbg_xcf", [128, NDB, LC], F32),
            ("dbg_dt", [128, NDB, LC], F32), ("dbg_yz", [128, NDB, LC], BF16),
            ("dbg_hf", [128, NIB, LC], BF16), ("dbg_ar", [160, LC], F32),
            ("dbg_rs", [2 * DM, TSL], F32), ("dbg_fo", [128, NHB, TSL], F32),
            ("dbg_carry", [128, NDB, DS], F32), ("dbg_aneg", [128, NDB, DS], F32),
            ("dbg_xld", [128, 2, DM], F32), ("dbg_xnt", [128, 2, DM], BF16),
            ("dbg_st", [128, 4, 6], F32), ("dbg_ag4", [128, 4, 2], F32),
            ("dbg_sm", [128, 2], F32), ("dbg_var", [128, 2], F32),
            ("dbg_h", [128, 4, LC], F32),
            ("dbg_rstd", [128, 2], F32), ("dbg_nmr", [128, 2], F32),
            ("dbg_aggr", [128, 2], F32),
        ]:
            dbg[name] = nc.dram_tensor(name, shape, dt_, kind="ExternalOutput").ap()

    d_winbf = nc.dram_tensor("i_winbf", [16, 128, NKB * 128], BF16, kind="Internal").ap()
    d_opbf = nc.dram_tensor("i_opbf", [16, 128, NDB * 128], BF16, kind="Internal").ap()
    d_gbf = nc.dram_tensor("i_gbf", [NIB, 128, NKB * 128], BF16, kind="Internal").ap()
    d_ubf = nc.dram_tensor("i_ubf", [NIB, 128, NKB * 128], BF16, kind="Internal").ap()
    d_dnbf = nc.dram_tensor("i_dnbf", [NIB, 2, 128, 1024], BF16, kind="Internal").ap()
    ar_in = [nc.dram_tensor(f"cc_ar_i{c}", [160, LC], F32, kind="Internal") for c in range(NCH)]
    ar_out = [nc.dram_tensor(f"cc_ar_o{c}", [160, LC], F32, kind="Internal") for c in range(NCH)]
    rs_in = [nc.dram_tensor(f"cc_rs_i{c}", [4 * 2 * DM, TSL], F32, kind="Internal") for c in range(NCH)]
    rs_out = [nc.dram_tensor(f"cc_rs_o{c}", [2 * DM, TSL], F32, kind="Internal") for c in range(NCH)]

    with contextlib.ExitStack() as ctx:
        E = ctx.enter_context

        s_convdiag = E(nc.sbuf_tensor("s_convdiag", [128, NDB, DC, 128], BF16))
        s_convb = E(nc.sbuf_tensor("s_convb", [128, NDB], F32))
        s_dtb2 = E(nc.sbuf_tensor("s_dtb2", [128, NDB], F32))
        s_dtbr = E(nc.sbuf_tensor("s_dtbr", [128, NDB], F32))
        s_aneg = E(nc.sbuf_tensor("s_aneg", [128, NDB, DS], F32))
        s_dvec = E(nc.sbuf_tensor("s_dvec", [128, NDB], F32))
        s_gsc = E(nc.sbuf_tensor("s_gsc", [128, NIB], F32))
        s_usc = E(nc.sbuf_tensor("s_usc", [128, NIB], F32))
        s_dsc = E(nc.sbuf_tensor("s_dsc", [128, NHB], F32))
        s_one = E(nc.sbuf_tensor("s_one", [128, 1], F32))
        s_eps = E(nc.sbuf_tensor("s_eps", [128, 1], F32))
        s_c128 = E(nc.sbuf_tensor("s_c128", [128, 1], F32))
        s_xprojT = E(nc.sbuf_tensor("s_xprojT", [128, NDB, 160], BF16))
        s_dtprojT = E(nc.sbuf_tensor("s_dtprojT", [128, DSL], BF16))
        s_xld = E(nc.sbuf_tensor("s_xld", [128, 2, DM], F32))
        s_xnt = E(nc.sbuf_tensor("s_xnt", [128, 2, DM], BF16))
        s_stats = E(nc.sbuf_tensor("s_stats", [128, 4, 6], F32))
        s_ag4 = E(nc.sbuf_tensor("s_ag4", [128, 4, 2], F32))
        s_msq = E(nc.sbuf_tensor("s_msq", [128, 4], F32))
        s_sm = E(nc.sbuf_tensor("s_sm", [128, 2], F32))
        s_var = E(nc.sbuf_tensor("s_var", [128, 2], F32))
        s_w1 = E(nc.sbuf_tensor("s_w1", [128, 1], F32))
        s_sp = E(nc.sbuf_tensor("s_sp", [128, NDB, LC], F32))
        s_w2 = E(nc.sbuf_tensor("s_w2", [128, 1], F32))
        s_rstd = E(nc.sbuf_tensor("s_rstd", [128, 2], F32))
        s_nmr = E(nc.sbuf_tensor("s_nmr", [128, 2], F32))
        s_r1 = E(nc.sbuf_tensor("s_r1", [128, 1], F32))
        s_xn = E(nc.sbuf_tensor("s_xn", [128, NKB, LC], BF16))
        s_xpre = E(nc.sbuf_tensor("s_xpre", [128, NDB, LC + 3], BF16))
        s_z = E(nc.sbuf_tensor("s_z", [128, NDB, LC], F32))
        s_xcf = E(nc.sbuf_tensor("s_xcf", [128, NDB, LC], F32))
        s_xcb = E(nc.sbuf_tensor("s_xcb", [128, NDB, LC], BF16))
        s_dt = E(nc.sbuf_tensor("s_dt", [128, NDB, LC], F32))
        s_dtlow = E(nc.sbuf_tensor("s_dtlow", [128, LC], BF16))
        s_xa = E(nc.sbuf_tensor("s_xa", [128, LC], F32))
        s_t1 = E(nc.sbuf_tensor("s_t1", [128, LC], F32))
        s_xb32 = s_t1[0:32, :]
        s_B = E(nc.sbuf_tensor("s_B", [128, DS, LC], F32))
        s_C = E(nc.sbuf_tensor("s_C", [128, DS, LC], F32))
        s_dA = E(nc.sbuf_tensor("s_dA", [128, 2, 8, LC], F32))
        s_dBu = E(nc.sbuf_tensor("s_dBu", [128, 4, LC], F32))
        s_h = E(nc.sbuf_tensor("s_h", [128, 4, LC], F32))
        s_p = E(nc.sbuf_tensor("s_p", [128, 4, LC], F32))
        s_carry = E(nc.sbuf_tensor("s_carry", [128, NDB, DS], F32))
        s_dtx = E(nc.sbuf_tensor("s_dtx", [128, LC], F32))
        s_yq = E(nc.sbuf_tensor("s_yq", [128, LC], F32))
        s_yt = E(nc.sbuf_tensor("s_yt", [128, LC], F32))
        s_yz = E(nc.sbuf_tensor("s_yz", [128, NDB, LC], BF16))
        s_hf = E(nc.sbuf_tensor("s_hf", [128, NIB, LC], BF16))
        s_sg = E(nc.sbuf_tensor("s_sg", [128, LC], F32))
        s_wslab = E(nc.sbuf_tensor("s_wslab", [128, 2, NKB, 128], BF16))
        s_wslab2 = E(nc.sbuf_tensor("s_wslab2", [128, 2, NKB, 128], BF16))
        s_dnslab = E(nc.sbuf_tensor("s_dnslab", [128, 2, 1024], BF16))
        s_opslab = E(nc.sbuf_tensor("s_opslab", [128, 8, NDB, 128], BF16))
        s_ev = E(nc.sbuf_tensor("s_ev", [128, 2, LC], F32))
        _finv = s_xld[:].rearrange("p s m -> p (s m)")

        def _fin_tile(j):
            return _finv[:, 1024 * j:1024 * (j + 1)].rearrange(
                "p (hb t) -> p hb t", hb=NHB)

        s_fs = _fin_tile(0)
        s_ff = _fin_tile(1)
        s_fx = _fin_tile(2)
        s_fg = _fin_tile(3)

        P = E(nc.psum_tensor("p_big", [128, 8, 512], F32))

        def pa(i):
            return P[:, i % 2, 0:LC]

        def pgk(i):
            return P[:, 2 + i % 2, 0:LC]

        def puk(i):
            return P[:, 4 + i % 2, 0:LC]

        def pdn(pp):
            return P[:, pp, 0:LC]

        sX = Sem(E(nc.semaphore("semX")))
        sW = Sem(E(nc.semaphore("semW")))
        sCC = Sem(E(nc.semaphore("semCC")))
        sPE = Sem(E(nc.semaphore("semPE")))
        sACT = Sem(E(nc.semaphore("semACT")))
        sDVE = Sem(E(nc.semaphore("semDVE")))
        sY = Sem(E(nc.semaphore("semY")))
        ALLS = (sX, sW, sCC, sPE, sACT, sDVE, sY)

        block = E(nc.Block())

        def wait_all(e):
            for s in ALLS:
                if s.n:
                    e.wait_ge(s.h, s.n)

        rot_pa = Rot(2)
        rot_gu = Rot(2)
        rot_slab = Rot(2)
        rot_dn = Rot(2)
        rot_op = Rot(8)
        rot_ev = Rot(2)
        rot_dA = Rot(2)

        def p0_consts(g):
            for dst, src in ((s_convb, d_convb), (s_dtbr, d_dtb), (s_dvec, d_dvec),
                             (s_gsc, d_gsc), (s_usc, d_usc), (s_dsc, d_dsc)):
                i = g.dma_start(out=dst[:], in_=src)
                sW.inc(i)
            i = g.dma_start(out=s_xld[:, 0, 0:NDB * DS], in_=d_alog.rearrange("p a b -> p (a b)"))
            sW.inc(i)

        block.gpsimd(p0_consts)

        def p0_dve(v):
            v.memset(s_one[:], 1.0)
            v.memset(s_eps[:], EPS)
            v.memset(s_c128[:], -128.0)
            v.memset(s_carry[:], 0.0)
            i = v.memset(s_xpre[:, :, 0:3], 0.0)
            sDVE.inc(i, 1)

        block.vector(p0_dve)

        def p0_act(a):
            wait_all(a)
            a.activation(out=s_dtb2[:], in_=s_dtbr[:], func=AF.Identity, scale=2.0)
            i = a.activation(out=s_xld[:, 1, 0:NDB * DS], in_=s_xld[:, 0, 0:NDB * DS], func=AF.Exp)
            sACT.inc(i, 1)

        block.scalar(p0_act)

        def p0_dve2(v):
            wait_all(v)
            i = v.tensor_scalar_mul(s_aneg[:].rearrange("p a b -> p (a b)"),
                                    s_xld[:, 1, 0:NDB * DS], -1.0)
            sDVE.inc(i, 1)

        block.vector(p0_dve2)

        cast_jobs = [
            (d_winT, d_winbf.rearrange("a b m -> (a b) m"), NKB, 2048, False),
            (d_gateT, d_gbf.rearrange("a b m -> (a b) m"), NKB, ISL, True),
            (d_upT, d_ubf.rearrange("a b m -> (a b) m"), NKB, ISL, True),
            (d_dnT, d_dnbf.rearrange("a c (b2 x) m -> (a c b2) (x m)", x=2), NIB, DM, True),
            (d_opT, d_opbf.rearrange("a (b2 x) m -> (a b2) (x m)", x=2), NDB, DM, False),
        ]
        cast_free = Rot(2)
        cast_nst = Rot(2)
        tcount = [0]
        for (src, dst, ntile, width, is_int) in cast_jobs:
            for t in range(ntile):
                sl = tcount[0] % 2
                tcount[0] += 1
                need_a = cast_free.free_at[sl]

                def g_load(g, src=src, t=t, sl=sl, width=width, is_int=is_int, need_a=need_a,
                           first=(tcount[0] == 1)):
                    if first:
                        wait_all(g)
                    if need_a:
                        g.wait_ge(sACT.h, need_a)
                    dstt = s_xld[:, sl, 0:width].bitcast(I32) if is_int else s_xld[:, sl, 0:width]
                    i = g.dma_start(out=dstt, in_=src[128 * t:128 * (t + 1), :])
                    sW.inc(i)

                block.gpsimd(g_load)
                need_w = sW.n
                need_st = cast_nst.free_at[sl]

                def a_cast(a, sl=sl, width=width, is_int=is_int, need_w=need_w, need_st=need_st):
                    a.wait_ge(sW.h, need_w)
                    if need_st:
                        a.wait_ge(sW.h, need_st)
                    if is_int:
                        i = a.activation(out=s_xnt[:, sl, 0:width], in_=s_xld[:, sl, 0:width].bitcast(I32),
                                         func=AF.Identity, bias=s_c128[:])
                    else:
                        i = a.activation(out=s_xnt[:, sl, 0:width], in_=s_xld[:, sl, 0:width],
                                         func=AF.Copy)
                    sACT.inc(i, 1)

                block.scalar(a_cast)
                cast_free.free_at[sl] = sACT.n
                need_a2 = sACT.n

                def g_store(g, dst=dst, t=t, sl=sl, width=width, need_a2=need_a2):
                    g.wait_ge(sACT.h, need_a2)
                    i = g.dma_start(out=dst[128 * t:128 * (t + 1), :], in_=s_xnt[:, sl, 0:width])
                    sW.inc(i)

                block.gpsimd(g_store)
                cast_nst.free_at[sl] = sW.n

        def p0_res_g(g):
            g.wait_ge(sACT.h, sACT.n)
            i = g.dma_start(out=s_xld[:, 0, 0:NDB * 160].rearrange("p (t m) -> p t m", t=NDB),
                            in_=d_xprojT.rearrange("(t p) m -> p t m", p=128))
            sW.inc(i)
            i = g.dma_start(out=s_xld[:, 1, 0:1024], in_=d_dtprojT)
            sW.inc(i)

        block.gpsimd(p0_res_g)

        def p0_res_a(a):
            a.wait_ge(sW.h, sW.n)
            a.activation(out=s_xprojT[:].rearrange("p t m -> p (t m)"),
                         in_=s_xld[:, 0, 0:NDB * 160], func=AF.Copy)
            i = a.activation(out=s_dtprojT[:], in_=s_xld[:, 1, 0:1024], func=AF.Copy)
            sACT.inc(i, 1)

        block.scalar(p0_res_a)

        def p0_res2_g(g):
            g.wait_ge(sACT.h, sACT.n)
            i = g.dma_start(out=s_xld[:].rearrange("p s m -> p (s m)"),
                            in_=d_convdiag.rearrange("p a b m -> p (a b m)"))
            sW.inc(i)

        block.gpsimd(p0_res2_g)

        def p0_res2_a(a):
            a.wait_ge(sW.h, sW.n)
            i = a.activation(out=s_convdiag[:].rearrange("p a b m -> p (a b m)"),
                             in_=s_xld[:].rearrange("p s m -> p (s m)"), func=AF.Copy)
            sACT.inc(i, 1)

        block.scalar(p0_res2_a)

        prev_dn_evac = 0
        for c in range(NCH):
            tok0 = c * LC

            def ph_ln_load(g, tok0=tok0):
                wait_all(g)
                for t in range(2):
                    i = g.dma_start(out=s_xld[:, t, :],
                                    in_=d_x[tok0 + 128 * t: tok0 + 128 * (t + 1), :])
                    sW.inc(i)

            block.gpsimd(ph_ln_load)

            def ph_ln_stats(v):
                wait_all(v)
                for t in range(2):
                    for q in range(4):
                        v.bn_stats(s_stats[:, q, :], s_xld[:, t, 512 * q: 512 * (q + 1)])
                    for q in range(4):
                        i = v.bn_aggr(s_ag4[:, q, :], s_stats[:, q, :])
                    sDVE.inc(i, 1)
                    v.wait_ge(sDVE.h, sDVE.n)
                    mview = s_ag4[:, :, 0:1].rearrange("p a b -> p (a b)")
                    vview = s_ag4[:, :, 1:2].rearrange("p a b -> p (a b)")
                    v.tensor_reduce(out=s_sm[:, t: t + 1], in_=mview, axis=AX.X, op=ALU.add)
                    v.tensor_reduce(out=s_w1[:], in_=vview, axis=AX.X, op=ALU.add)
                    i = v.tensor_tensor(out=s_msq[:], in0=mview, in1=mview, op=ALU.mult)
                    sDVE.inc(i, 1)
                    v.wait_ge(sDVE.h, sDVE.n)
                    v.tensor_reduce(out=s_w2[:], in_=s_msq[:], axis=AX.X, op=ALU.add)
                    i = v.tensor_tensor(out=s_msq[:, 0:1], in0=s_sm[:, t: t + 1],
                                        in1=s_sm[:, t: t + 1], op=ALU.mult)
                    sDVE.inc(i, 1)
                    v.wait_ge(sDVE.h, sDVE.n)
                    i = v.scalar_tensor_tensor(out=s_w1[:], in0=s_msq[:, 0:1], scalar=-0.25,
                                               in1=s_w1[:], op0=ALU.mult, op1=ALU.add)
                    sDVE.inc(i, 1)
                    v.wait_ge(sDVE.h, sDVE.n)
                    i = v.scalar_tensor_tensor(out=s_var[:, t: t + 1], in0=s_w2[:], scalar=1.0,
                                               in1=s_w1[:], op0=ALU.mult, op1=ALU.add)
                    sDVE.inc(i, 1)
                    v.wait_ge(sDVE.h, sDVE.n)
                    i = v.tensor_scalar(out=s_var[:, t: t + 1], in0=s_var[:, t: t + 1],
                                        scalar1=0.25, scalar2=None, op0=ALU.mult)
                sDVE.inc(i, 1)

            block.vector(ph_ln_stats)

            def ph_ln_rstd(a):
                a.wait_ge(sDVE.h, sDVE.n)
                for t in range(2):
                    a.activation(out=s_r1[:], in_=s_var[:, t: t + 1], func=AF.Ln, bias=s_eps[:])
                    i = a.activation(out=s_rstd[:, t: t + 1], in_=s_r1[:], func=AF.Exp, scale=-0.5)
                sACT.inc(i, 1)

            block.scalar(ph_ln_rstd)

            def ph_ln_nmr(v):
                v.wait_ge(sACT.h, sACT.n)
                for t in range(2):
                    i = v.tensor_scalar(out=s_nmr[:, t: t + 1], in0=s_sm[:, t: t + 1],
                                        scalar1=s_rstd[:, t: t + 1], scalar2=-0.25,
                                        op0=ALU.mult, op1=ALU.mult)
                sDVE.inc(i, 1)

            block.vector(ph_ln_nmr)

            def ph_ln_apply(a):
                a.wait_ge(sDVE.h, sDVE.n)
                for t in range(2):
                    i = a.activation(out=s_xnt[:, t, :], in_=s_xld[:, t, :], func=AF.Identity,
                                     scale=s_rstd[:, t: t + 1], bias=s_nmr[:, t: t + 1])
                sACT.inc(i, 1)

            block.scalar(ph_ln_apply)

            def ph_ln_tr(sy):
                wait_all(sy)
                for t in range(2):
                    i = sy.dma_start_transpose(out=s_xn[:, :, 128 * t: 128 * (t + 1)],
                                               in_=s_xnt[:, t, :])
                    sX.inc(i)

            block.sync(ph_ln_tr)
            lntr_cnt = sX.n

            for pout in range(16):
                sl = pout % 2
                need_slab_free = rot_slab.free_at[sl]

                def g_slab(g, pout=pout, sl=sl, need=need_slab_free, first=(pout == 0)):
                    if first:
                        wait_all(g)
                    if need:
                        g.wait_ge(sPE.h, need)
                    i = g.dma_start(out=s_wslab[:, sl, :, :],
                                    in_=d_winbf[pout].rearrange("p (kb m) -> p kb m", kb=NKB))
                    sW.inc(i)

                block.gpsimd(g_slab)
                need_w = sW.n
                need_free = rot_pa.free_at[pout % 2]

                def pe_mm(te, pout=pout, sl=sl, need_w=need_w, need_free=need_free,
                          first=(pout == 0)):
                    if first:
                        wait_all(te)
                    te.wait_ge(sW.h, need_w)
                    if need_free:
                        te.wait_ge(sACT.h, need_free)
                    ps = pa(pout)
                    for k in range(NKB):
                        i = te.matmul(ps, s_wslab[:, sl, k, :], s_xn[:, k, :],
                                      start=(k == 0), stop=(k == NKB - 1))
                    sPE.inc(i, 1)

                block.tensor(pe_mm)
                rot_slab.free_at[sl] = sPE.n
                rot_pa.ready_at[pout % 2] = sPE.n
                need_pe = sPE.n

                def a_evac(a, pout=pout, need_pe=need_pe):
                    a.wait_ge(sPE.h, need_pe)
                    ps = pa(pout)
                    if pout < 8:
                        i = a.activation(out=s_xpre[:, pout, 3:3 + LC], in_=ps, func=AF.Copy)
                    else:
                        i = a.activation(out=s_z[:, pout - 8, :], in_=ps, func=AF.Copy)
                    sACT.inc(i, 1)

                block.scalar(a_evac)
                rot_pa.free_at[pout % 2] = sACT.n

            for d in range(NDB):
                need_free = rot_pa.free_at[d % 2]

                def pe_conv(te, d=d, need_free=need_free):
                    if d == 0:
                        wait_all(te)
                    if need_free:
                        te.wait_ge(sACT.h, need_free)
                    ps = pa(d)
                    for k in range(DC):
                        i = te.matmul(ps, s_convdiag[:, d, k, :],
                                      s_xpre[:, d, k: k + LC],
                                      start=(k == 0), stop=(k == DC - 1))
                    sPE.inc(i, 1)

                block.tensor(pe_conv)
                need_pe = sPE.n

                def a_conv(a, d=d, need_pe=need_pe):
                    a.wait_ge(sPE.h, need_pe)
                    a.activation(out=s_xcf[:, d, :], in_=pa(d), func=AF.Silu,
                                 bias=s_convb[:, d: d + 1])
                    i = a.activation(out=s_xcb[:, d, :], in_=s_xcf[:, d, :], func=AF.Copy)
                    sACT.inc(i, 1)

                block.scalar(a_conv)
                rot_pa.free_at[d % 2] = sACT.n

            def v_halo(v):
                wait_all(v)
                i = v.tensor_copy(s_xpre[:, :, 0:3], s_xpre[:, :, LC:LC + 3])
                sDVE.inc(i, 1)

            block.vector(v_halo)

            def ph_xproj(te):
                wait_all(te)
                for k in range(NDB):
                    te.matmul(P[:, 0, 0:LC], s_xprojT[:, k, 0:128], s_xcb[:, k, :],
                              start=(k == 0), stop=(k == NDB - 1))
                for k in range(NDB):
                    i = te.matmul(P[0:32, 1, 0:LC], s_xprojT[:, k, 128:160], s_xcb[:, k, :],
                                  start=(k == 0), stop=(k == NDB - 1))
                sPE.inc(i, 1)

            block.tensor(ph_xproj)

            def ph_xproj_ev(a):
                a.wait_ge(sPE.h, sPE.n)
                a.activation(out=s_xa[:], in_=P[:, 0, 0:LC], func=AF.Copy)
                i = a.activation(out=s_xb32, in_=P[0:32, 1, 0:LC], func=AF.Copy)
                sACT.inc(i, 1)

            block.scalar(ph_xproj_ev)
            rot_pa.free_at[0] = sACT.n
            rot_pa.free_at[1] = sACT.n

            def ph_ar(g, c=c):
                g.wait_ge(sACT.h, sACT.n)
                i = g.dma_start(out=ar_in[c].ap()[0:128, :], in_=s_xa[:])
                sW.inc(i)
                i = g.dma_start(out=ar_in[c].ap()[128:160, :], in_=s_xb32)
                sW.inc(i)
                g.wait_ge(sW.h, sW.n)
                i = g.collective_compute("AllReduce", ALU.add, replica_groups=RG,
                                         ins=[ar_in[c].ap().opt()], outs=[ar_out[c].ap().opt()])
                sCC.inc(i, 1)
                g.wait_ge(sCC.h, sCC.n)
                i = g.dma_start(out=s_t1[:], in_=ar_out[c].ap()[0:128, :])
                sW.inc(i)
                src_b = ar_out[c].ap()[128:144, :].rearrange("(o n) l -> o n l", o=1).broadcast_to(
                    [128, DS, LC])
                src_c = ar_out[c].ap()[144:160, :].rearrange("(o n) l -> o n l", o=1).broadcast_to(
                    [128, DS, LC])
                i = g.dma_start(out=s_B[:], in_=src_b)
                sW.inc(i)
                i = g.dma_start(out=s_C[:], in_=src_c)
                sW.inc(i)

            block.gpsimd(ph_ar)

            need_bc = sW.n

            op_slab_cnt = [0] * NHB

            def emit_g_op(pout):
                osl = pout % 8
                need_op = rot_op.free_at[osl]

                def g_op(g, pout=pout, osl=osl, need=need_op):
                    if need:
                        g.wait_ge(sPE.h, need)
                    i = g.dma_start(out=s_opslab[:, osl, :, :],
                                    in_=d_opbf[pout].rearrange("p (kb m) -> p kb m", kb=NDB))
                    sW.inc(i)

                block.gpsimd(g_op)
                op_slab_cnt[pout] = sW.n

            for pout in range(8):
                emit_g_op(pout)

            for ib in range(NIB):
                sl = ib % 2
                need_slab = rot_slab.free_at[sl]

                def g_gu(sy, ib=ib, sl=sl, need=need_slab):
                    if need:
                        sy.wait_ge(sPE.h, need)
                    i = sy.dma_start(out=s_wslab[:, sl, :, :],
                                     in_=d_gbf[ib].rearrange("p (kb m) -> p kb m", kb=NKB))
                    sY.inc(i)
                    i = sy.dma_start(out=s_wslab2[:, sl, :, :],
                                     in_=d_ubf[ib].rearrange("p (kb m) -> p kb m", kb=NKB))
                    sY.inc(i)

                block.sync(g_gu)
                need_w = sY.n
                need_gu_free = rot_gu.free_at[ib % 2]

                def pe_gu(te, ib=ib, sl=sl, need_w=need_w, need_gu=need_gu_free,
                          first=(ib == 0), pde=prev_dn_evac, lntr=lntr_cnt):
                    if first:
                        if pde:
                            te.wait_ge(sACT.h, pde)
                        te.wait_ge(sX.h, lntr)
                    te.wait_ge(sY.h, need_w)
                    if need_gu:
                        te.wait_ge(sDVE.h, need_gu)
                    for k in range(NKB):
                        te.matmul(pgk(ib), s_wslab[:, sl, k, :], s_xn[:, k, :],
                                  start=(k == 0), stop=(k == NKB - 1))
                    for k in range(NKB):
                        i = te.matmul(puk(ib), s_wslab2[:, sl, k, :], s_xn[:, k, :],
                                      start=(k == 0), stop=(k == NKB - 1))
                    sPE.inc(i, 1)

                block.tensor(pe_gu)
                rot_slab.free_at[sl] = sPE.n
                need_pe = sPE.n
                need_sg_free = sDVE.n

                def a_silu(a, ib=ib, need_pe=need_pe, need_sg=need_sg_free):
                    a.wait_ge(sPE.h, need_pe)
                    if need_sg:
                        a.wait_ge(sDVE.h, need_sg)
                    i = a.activation(out=s_sg[:], in_=pgk(ib), func=AF.Silu,
                                     scale=s_gsc[:, ib: ib + 1])
                    sACT.inc(i, 1)

                block.scalar(a_silu)
                need_a = sACT.n

                def v_h(v, ib=ib, need_a=need_a):
                    v.wait_ge(sACT.h, need_a)
                    i = v.scalar_tensor_tensor(out=s_hf[:, ib, :], in0=puk(ib),
                                               scalar=s_usc[:, ib: ib + 1], in1=s_sg[:],
                                               op0=ALU.mult, op1=ALU.mult)
                    sDVE.inc(i, 1)

                block.vector(v_h)
                rot_gu.free_at[ib % 2] = sDVE.n
            vh15 = sDVE.n

            def ph_dtlow(a, need_bc=need_bc):
                a.wait_ge(sW.h, need_bc)
                i = a.activation(out=s_dtlow[:], in_=s_t1[:], func=AF.Copy)
                sACT.inc(i, 1)

            block.scalar(ph_dtlow)

            DTV = s_dA[:, 0, :, :]
            for d in range(NDB):
                need_free = rot_pa.free_at[d % 2]

                def pe_dt(te, d=d, need_free=need_free):
                    if d == 0:
                        te.wait_ge(sACT.h, sACT.n)
                        wait_all(te)
                    if need_free:
                        te.wait_ge(sACT.h, need_free)
                    i = te.matmul(pa(d), s_dtprojT[:, 128 * d:128 * (d + 1)], s_dtlow[:],
                                  start=True, stop=True)
                    sPE.inc(i, 1)

                block.tensor(pe_dt)
                need_pe = sPE.n

                def a_dtev(a, d=d, need_pe=need_pe):
                    a.wait_ge(sPE.h, need_pe)
                    i = a.activation(out=DTV[:, d, :], in_=pa(d), func=AF.Identity,
                                     bias=s_dtb2[:, d: d + 1])
                    sACT.inc(i, 1)

                block.scalar(a_dtev)
                rot_pa.free_at[d % 2] = sACT.n
            dtev_last = sACT.n

            def scr(d):
                return s_dBu[:, d, :] if d < 4 else s_h[:, d - 4, 0:LC]

            def a_softplus(a):
                for d in range(NDB):
                    a.activation(out=s_sp[:, d, :], in_=DTV[:, d, :], func=AF.Relu)
                for d in range(NDB):
                    a.activation(out=scr(d), in_=DTV[:, d, :], func=AF.Abs)
                for d in range(NDB):
                    a.activation(out=DTV[:, d, :], in_=scr(d), func=AF.Exp, scale=-1.0)
                for d in range(NDB):
                    i = a.activation(out=scr(d), in_=DTV[:, d, :], func=AF.Ln, bias=s_one[:])
                sACT.inc(i, 1)

            block.scalar(a_softplus)
            need_a_sp = sACT.n

            def v_dt(v, need_a_sp=need_a_sp):
                v.wait_ge(sACT.h, need_a_sp)
                for d in range(NDB):
                    i = v.tensor_tensor(out=s_dt[:, d, :], in0=s_sp[:, d, :], in1=scr(d),
                                        op=ALU.add)
                sDVE.inc(i, 1)

            block.vector(v_dt)
            v_dt_mark = sDVE.n

            dn_evac_cnt = [0] * NHB

            def emit_dn_mms(pg, need_ev):
                for k in range(NIB):
                    sl = k % 2
                    need_dn = rot_dn.free_at[sl]

                    def g_dn(sy, pg=pg, k=k, sl=sl, need=need_dn):
                        if need:
                            sy.wait_ge(sPE.h, need)
                        i = sy.dma_start(out=s_dnslab[:, sl, :], in_=d_dnbf[k, pg])
                        sY.inc(i)

                    block.sync(g_dn)
                    need_w = sY.n

                    def pe_dn(te, pg=pg, k=k, sl=sl, need_w=need_w, need_ev=need_ev,
                              vh=vh15, dte=dtev_last):
                        te.wait_ge(sY.h, need_w)
                        if k == 0:
                            te.wait_ge(sDVE.h, vh)
                            te.wait_ge(sACT.h, dte if pg == 0 else need_ev)
                        for pp in range(8):
                            i = te.matmul(pdn(pp), s_dnslab[:, sl, 128 * pp:128 * (pp + 1)],
                                          s_hf[:, k, :], start=(k == 0), stop=(k == NIB - 1))
                        sPE.inc(i, 1)

                    block.tensor(pe_dn)
                    rot_dn.free_at[sl] = sPE.n
                return sPE.n

            def emit_dn_evacs(pg, need_pe, c=c):
                for pp in range(8):
                    pout = 8 * pg + pp
                    esl = pout % 2
                    need_ev_free = rot_ev.free_at[esl]

                    def a_dn(a, pp=pp, pout=pout, esl=esl, need_pe=need_pe,
                             need=need_ev_free):
                        a.wait_ge(sPE.h, need_pe)
                        if need:
                            a.wait_ge(sW.h, need)
                        i = a.activation(out=s_ev[:, esl, :], in_=pdn(pp), func=AF.Copy,
                                         scale=s_dsc[:, pout: pout + 1])
                        sACT.inc(i, 1)

                    block.scalar(a_dn)
                    dn_evac_cnt[pout] = sACT.n

                    def g_f(g, pout=pout, esl=esl, c=c, need_a=sACT.n):
                        g.wait_ge(sACT.h, need_a)
                        dst = rs_in[c].ap().rearrange("(r hb p) t -> p r hb t", r=4, p=128)[
                            :, :, NHB + pout, :]
                        i = g.dma_start(out=dst,
                                        in_=s_ev[:, esl, :].rearrange("p (r t) -> p r t", r=4))
                        sW.inc(i)

                    block.gpsimd(g_f)
                    rot_ev.free_at[esl] = sW.n
                return sACT.n

            def emit_scan_d(d):
                cnt_ab = [0, 0]
                for h in range(2):
                    need_dA_free = max(rot_dA.free_at[h], v_dt_mark)

                    def a_dA(a, d=d, h=h, need=need_dA_free):
                        a.wait_ge(sDVE.h, need)
                        for j in range(8):
                            i = a.activation(out=s_dA[:, h, j, :], in_=s_dt[:, d, :],
                                             func=AF.Exp,
                                             scale=s_aneg[:, d, 8 * h + j: 8 * h + j + 1])
                        sACT.inc(i, 1)

                    block.scalar(a_dA)
                    cnt_ab[h] = sACT.n

                def v_scan(v, d=d, cnt_ab=tuple(cnt_ab), need_bc=need_bc, first=(d == 0)):
                    v.wait_ge(sACT.h, cnt_ab[0])
                    if first:
                        v.wait_ge(sW.h, need_bc)
                    v.tensor_tensor(out=s_dtx[:], in0=s_dt[:, d, :], in1=s_xcf[:, d, :],
                                    op=ALU.mult)
                    dtxv = s_dtx[:].rearrange("p (o l) -> p o l", o=1).broadcast_to([128, 4, LC])
                    for q in range(4):
                        if q == 2:
                            v.wait_ge(sACT.h, cnt_ab[1])
                        v.tensor_tensor(out=s_dBu[:], in0=dtxv, in1=s_B[:, 4 * q:4 * q + 4, :],
                                        op=ALU.mult)
                        for n4 in range(4):
                            n = 4 * q + n4
                            v.tensor_tensor_scan(out=s_h[:, n4, 0:LC],
                                                 data0=s_dA[:, q // 2, n % 8, :],
                                                 data1=s_dBu[:, n4, :],
                                                 initial=s_carry[:, d, n: n + 1],
                                                 op0=ALU.mult, op1=ALU.add)
                        v.tensor_copy(s_carry[:, d, 4 * q:4 * q + 4],
                                      s_h[:, :, LC - 1:LC].rearrange("p n o -> p (n o)"))
                        v.tensor_tensor(out=s_p[:], in0=s_h[:, :, 0:LC],
                                        in1=s_C[:, 4 * q:4 * q + 4, :], op=ALU.mult)
                        pv = s_p[:].rearrange("p n l -> p l n")
                        if q == 0:
                            v.tensor_reduce(out=s_yq[:], in_=pv, axis=AX.X, op=ALU.add)
                        else:
                            v.tensor_reduce(out=s_yt[:], in_=pv, axis=AX.X, op=ALU.add)
                            i = v.tensor_tensor(out=s_yq[:], in0=s_yq[:], in1=s_yt[:],
                                                op=ALU.add)
                        if q == 1:
                            sDVE.inc(i, 1)
                            rot_dA.free_at[0] = sDVE.n
                    v.scalar_tensor_tensor(out=s_yt[:], in0=s_xcf[:, d, :],
                                           scalar=s_dvec[:, d: d + 1], in1=s_yq[:],
                                           op0=ALU.mult, op1=ALU.add)
                    i = v.tensor_tensor(out=s_yz[:, d, :], in0=s_yt[:], in1=s_z[:, d, :],
                                        op=ALU.mult)
                    sDVE.inc(i, 1)
                    rot_dA.free_at[1] = sDVE.n

                block.vector(v_scan)

            pg0_done = emit_dn_mms(0, 0)
            for d in range(4):
                emit_scan_d(d)
            pg0_evac_last = emit_dn_evacs(0, pg0_done)
            pg1_done = emit_dn_mms(1, pg0_evac_last)
            for d in range(4, 8):
                emit_scan_d(d)
            scan_done = sDVE.n
            pg1_evac_last = emit_dn_evacs(1, pg1_done)
            prev_dn_evac = pg1_evac_last

            for pout in range(NHB):
                osl = pout % 8
                need_free = rot_pa.free_at[pout % 2]

                def pe_op(te, pout=pout, osl=osl, need_w=op_slab_cnt[pout],
                          need_free=need_free,
                          first=(pout == 0), sd=scan_done, pge=pg1_evac_last):
                    if first:
                        te.wait_ge(sDVE.h, sd)
                        te.wait_ge(sACT.h, pge)
                    te.wait_ge(sW.h, need_w)
                    if need_free:
                        te.wait_ge(sACT.h, need_free)
                    for k in range(NDB):
                        i = te.matmul(pa(pout), s_opslab[:, osl, k, :], s_yz[:, k, :],
                                      start=(k == 0), stop=(k == NDB - 1))
                    sPE.inc(i, 1)

                block.tensor(pe_op)
                rot_op.free_at[osl] = sPE.n
                if pout < 8:
                    emit_g_op(pout + 8)
                need_pe = sPE.n
                esl = pout % 2
                need_ev_free = rot_ev.free_at[esl]

                def a_op(a, pout=pout, esl=esl, need_pe=need_pe, need=need_ev_free):
                    a.wait_ge(sPE.h, need_pe)
                    if need:
                        a.wait_ge(sW.h, need)
                    i = a.activation(out=s_ev[:, esl, :], in_=pa(pout), func=AF.Copy)
                    sACT.inc(i, 1)

                block.scalar(a_op)
                rot_pa.free_at[pout % 2] = sACT.n
                need_a = sACT.n

                def g_s(g, pout=pout, esl=esl, c=c, need_a=need_a):
                    g.wait_ge(sACT.h, need_a)
                    dst = rs_in[c].ap().rearrange("(r hb p) t -> p r hb t", r=4, p=128)[
                        :, :, pout, :]
                    i = g.dma_start(out=dst,
                                    in_=s_ev[:, esl, :].rearrange("p (r t) -> p r t", r=4))
                    sW.inc(i)

                block.gpsimd(g_s)
                rot_ev.free_at[esl] = sW.n

            def g_rs(g, c=c):
                g.wait_ge(sW.h, sW.n)
                i = g.collective_compute("ReduceScatter", ALU.add, replica_groups=RG,
                                         ins=[rs_in[c].ap().opt()], outs=[rs_out[c].ap().opt()])
                sCC.inc(i, 1)

            block.gpsimd(g_rs)

            def g_fin(g, c=c):
                g.wait_ge(sCC.h, sCC.n)
                wait_all(g)
                i = g.dma_start(out=s_fs, in_=rs_out[c].ap()[0:DM, :].rearrange(
                    "(hb p) t -> p hb t", p=128))
                sW.inc(i)
                i = g.dma_start(out=s_ff, in_=rs_out[c].ap()[DM:2 * DM, :].rearrange(
                    "(hb p) t -> p hb t", p=128))
                sW.inc(i)
                i = g.dma_start(out=s_fx, in_=d_xfinT[c].rearrange("(hb p) t -> p hb t", p=128))
                sW.inc(i)

            block.gpsimd(g_fin)

            def a_fin(a):
                a.wait_ge(sW.h, sW.n)
                i = a.activation(out=s_fg.rearrange("p a b -> p (a b)"),
                                 in_=s_ff.rearrange("p a b -> p (a b)"), func=AF.Silu)
                sACT.inc(i, 1)

            block.scalar(a_fin)

            def v_fin(v):
                v.wait_ge(sACT.h, sACT.n)
                v.tensor_tensor(out=s_ff, in0=s_fs, in1=s_fg, op=ALU.mult)
                i = v.tensor_tensor(out=s_fs, in0=s_ff, in1=s_fx, op=ALU.add)
                sDVE.inc(i, 1)

            block.vector(v_fin)

            def sy_out(sy, c=c):
                sy.wait_ge(sDVE.h, sDVE.n)
                i = sy.dma_start(out=d_out[c].rearrange("(hb p) t -> p hb t", p=128), in_=s_fs)
                sX.inc(i)

            block.sync(sy_out)

            if DEBUG and c == 0:
                def g_dbg(g):
                    wait_all(g)
                    pairs = [("dbg_xn", s_xn[:]), ("dbg_xpre", s_xpre[:]), ("dbg_z", s_z[:]),
                             ("dbg_xcf", s_xcf[:]), ("dbg_dt", s_dt[:]), ("dbg_yz", s_yz[:]),
                             ("dbg_hf", s_hf[:]), ("dbg_fo", s_fs), ("dbg_carry", s_carry[:]),
                             ("dbg_aneg", s_aneg[:]), ("dbg_xld", s_xld[:]),
                             ("dbg_xnt", s_xnt[:]), ("dbg_rstd", s_rstd[:]),
                             ("dbg_nmr", s_nmr[:]),
                             ("dbg_st", s_stats[:]), ("dbg_ag4", s_ag4[:]),
                             ("dbg_sm", s_sm[:]), ("dbg_var", s_var[:]),
                             ("dbg_h", s_h[:, :, 0:LC])]
                    for name, ap in pairs:
                        i = g.dma_start(out=dbg[name], in_=ap)
                        sW.inc(i)
                    i = g.dma_start(out=dbg["dbg_ar"], in_=ar_out[0].ap())
                    sW.inc(i)
                    i = g.dma_start(out=dbg["dbg_rs"], in_=rs_out[0].ap())
                    sW.inc(i)
                    g.wait_ge(sW.h, sW.n)

                block.gpsimd(g_dbg)

        def g_end(g):
            g.wait_ge(sX.h, sX.n)
            g.wait_ge(sW.h, sW.n)

        block.gpsimd(g_end)

    return nc


def kernel(**inputs):
    x = np.ascontiguousarray(np.asarray(inputs["L0_attn_hidden_states"], np.float32))
    w_in = np.asarray(inputs["L0_attn_in_proj_w"], np.float32)
    conv_w = np.asarray(inputs["L0_attn_conv_w"], np.float32)
    conv_b = np.asarray(inputs["L0_attn_conv_b"], np.float32)
    xproj = np.asarray(inputs["L0_attn_x_proj_w"], np.float32)
    dtproj = np.asarray(inputs["L0_attn_dt_proj_w"], np.float32)
    dtb = np.asarray(inputs["L0_attn_dt_proj_b"], np.float32)
    alog = np.asarray(inputs["L0_attn_A_log"], np.float32)
    dvec = np.asarray(inputs["L0_attn_D"], np.float32)
    op_w = np.asarray(inputs["L0_attn_out_proj_w"], np.float32)
    gate_w = np.asarray(inputs["L0_ffn_gate_w"], np.int32)
    gate_s = np.asarray(inputs["L0_ffn_gate_scale"], np.float32)
    up_w = np.asarray(inputs["L0_ffn_up_w"], np.int32)
    up_s = np.asarray(inputs["L0_ffn_up_scale"], np.float32)
    dn_w = np.asarray(inputs["L0_ffn_down_w"], np.int32)
    dn_s = np.asarray(inputs["L0_ffn_down_scale"], np.float32)

    def arr128(v):
        return np.ascontiguousarray(v.reshape(NDB, 128).T)

    def slab16(w):
        kb = w.shape[0] // 128
        po = w.shape[1] // 128
        return np.ascontiguousarray(
            w.reshape(kb, 128, po, 128).transpose(2, 1, 0, 3).reshape(w.shape))

    def slabdn(w):
        return np.ascontiguousarray(
            w.reshape(16, 128, 2, 1024).transpose(0, 2, 1, 3).reshape(w.shape))

    in_maps = []
    for core in range(8):
        b, j = divmod(core, 4)
        Dj = slice(DSL * j, DSL * (j + 1))
        Ij = slice(ISL * j, ISL * (j + 1))
        xb = x[b]
        cw = conv_w[Dj]
        diag = np.zeros((128, NDB, DC, 128), np.float32)
        for t in range(NDB):
            for k in range(DC):
                np.fill_diagonal(diag[:, t, k, :], cw[128 * t:128 * (t + 1), k])
        xfin = np.stack([
            np.ascontiguousarray(xb[LC * c + TSL * j: LC * c + TSL * (j + 1), :].T)
            for c in range(NCH)])
        in_maps.append({
            "x_b": xb,
            "w_inT": slab16(
                np.concatenate([w_in[Dj], w_in[DI + DSL * j: DI + DSL * (j + 1)]], axis=0).T),
            "convdiag": diag,
            "convb": arr128(conv_b[Dj]),
            "xprojT": np.ascontiguousarray(xproj[:, Dj].T),
            "dtprojT": np.ascontiguousarray(dtproj[Dj].T),
            "dtb": arr128(dtb[Dj]),
            "alog": np.ascontiguousarray(alog[Dj].reshape(NDB, 128, DS).transpose(1, 0, 2)),
            "dvec": arr128(dvec[Dj]),
            "opT": slab16(np.ascontiguousarray(op_w[:, Dj].T)),
            "gateT": slab16(np.ascontiguousarray(gate_w[Ij].T)),
            "gsc": np.ascontiguousarray(gate_s[Ij].reshape(NIB, 128).T),
            "upT": slab16(np.ascontiguousarray(up_w[Ij].T)),
            "usc": np.ascontiguousarray(up_s[Ij].reshape(NIB, 128).T),
            "dnT": slabdn(np.ascontiguousarray(dn_w[:, Ij].T)),
            "dsc": np.ascontiguousarray(dn_s.reshape(NHB, 128).T),
            "xfinT": xfin,
        })

    nc = build_nc()
    res = run_bass_kernel_spmd(nc, in_maps, core_ids=list(range(8)))
    global _last_results
    _last_results = res.results

    out = np.empty((B, L, DM), np.float32)
    for core in range(8):
        b, j = divmod(core, 4)
        oq = np.asarray(res.results[core]["out_q"])
        for c in range(NCH):
            out[b, LC * c + TSL * j: LC * c + TSL * (j + 1), :] = oq[c].T
    return out
